# version 04
# baseline (speedup 1.0000x reference)
import numpy as np
import ml_dtypes
import contextlib
import concourse.bass as bass
import concourse.mybir as mybir
from concourse.alu_op_type import AluOpType
from concourse.bass_utils import run_bass_kernel_spmd

F32 = mybir.dt.float32
BF16 = mybir.dt.bfloat16
I32 = mybir.dt.int32
AF = mybir.ActivationFunctionType
OP = AluOpType

TOK = 1024
HALO = 512
D = 4096
HC = 32
NQ, NKV, HD = 32, 8, 128
II = 11008
IC = 86
SCALE = float(1.0 / np.sqrt(128.0))
EPS = 1e-5
KEYS = TOK + HALO
KC = 12


def build_core_program():
    nc = bass.Bass("TRN2", target_bir_lowering=False)

    xm = nc.dram_tensor("xm", [TOK, D], F32, kind="ExternalInput").ap()
    xh = nc.dram_tensor("xh", [HALO, D], F32, kind="ExternalInput").ap()
    wq4 = nc.dram_tensor("wq4", [NQ, 128, HC, 128], F32, kind="ExternalInput").ap()
    wk4 = nc.dram_tensor("wk4", [NKV, 128, HC, 128], F32, kind="ExternalInput").ap()
    wv4 = nc.dram_tensor("wv4", [NKV, 128, HC, 128], F32, kind="ExternalInput").ap()
    wo4 = nc.dram_tensor("wo4", [8, 32, 128, 512], F32, kind="ExternalInput").ap()
    qg5 = nc.dram_tensor("qg5", [IC, 128, D], I32, kind="ExternalInput").ap()
    qu5 = nc.dram_tensor("qu5", [IC, 128, D], I32, kind="ExternalInput").ap()
    qd5 = nc.dram_tensor("qd5", [32, 128, II], I32, kind="ExternalInput").ap()
    szgu = nc.dram_tensor("szgu", [IC, 4, 128, 32], F32, kind="ExternalInput").ap()
    szd = nc.dram_tensor("szd", [16, 2, 2, 128, IC], F32, kind="ExternalInput").ap()
    maskd = nc.dram_tensor("maskd", [5, 128, 640], BF16, kind="ExternalInput").ap()
    identd = nc.dram_tensor("identd", [128, 128], BF16, kind="ExternalInput").ap()
    outd = nc.dram_tensor("outd", [TOK, D], F32, kind="ExternalOutput").ap()
    x1d = nc.dram_tensor("x1d", [TOK, D], F32).ap()
    hbd = nc.dram_tensor("hbd", [128, IC, 512], BF16).ap()

    st = contextlib.ExitStack()

    SEMS = ["t", "v", "a", "p"]
    H = {s: st.enter_context(nc.semaphore(s)) for s in SEMS}
    N = {s: 0 for s in SEMS}

    def inc(s, instr, k=1):
        instr.then_inc(H[s], k)
        N[s] += k
        return N[s]

    def wt(eng, s, target=None):
        if target is None:
            target = N[s]
        if target > 0:
            eng.wait_ge(H[s], target)

    dslots = {}

    def ddma(name, qeng, dst, src_):
        if name not in dslots:
            dslots[name] = [st.enter_context(nc.semaphore("D" + name)), 0]
        ent = dslots[name]
        qeng.dma_start(dst, src_).then_inc(ent[0], 16)
        ent[1] += 16
        return (name, ent[1])

    def dwt(eng, tag):
        if tag is None:
            return
        nm, target = tag
        eng.wait_ge(dslots[nm][0], target)

    guards = {}

    def salloc(name, shape, dt, side="left"):
        gd = nc.sbuf_tensor(name, shape, dt, side=side)
        h = gd.__enter__()
        guards[name] = gd
        return h

    def sfree(*names):
        for name in names:
            guards.pop(name).__exit__(None, None, None)

    psum = st.enter_context(nc.psum_tensor("ps", [128, 4096], F32))

    def bk(k, off=0, n=512):
        return psum[:, 512 * k + off: 512 * k + off + n]

    ident = salloc("ident", [128, 128], BF16)
    eps_sb = salloc("eps", [128, 1], F32)
    stat6 = salloc("stat6", [128, 48], F32)
    mv2 = salloc("mv2", [128, 2], F32)
    std1 = salloc("std1", [128, 1], F32)
    r1 = salloc("r1", [128, 1], F32)
    nmr1 = salloc("nmr1", [128, 1], F32)
    wos1 = salloc("wos1", [128, 64], F32)
    wos2 = salloc("wos2", [128, 64], F32)
    r2a = salloc("r2a", [128, 8], F32)
    nmr2a = salloc("nmr2a", [128, 8], F32)
    sum1 = salloc("sum1", [128, 1], F32)
    sum2 = salloc("sum2", [128, 1], F32)
    mu1 = salloc("mu1", [128, 1], F32)
    nzs = salloc("nzs", [128, 64], F32)

    v, a, t, g, sy = nc.vector, nc.scalar, nc.tensor, nc.gpsimd, nc.sync

    blk = st.enter_context(nc.Block())

    def fence():
        for eng in (t, v, a, g, sy):
            for s in ("v", "a", "t", "p"):
                wt(eng, s)

    def emit():
        cst_tag = ddma("cst", sy, ident[:], identd)
        dwt(v, cst_tag)
        v.memset(eps_sb[:], EPS)
        inc("v", v.tensor_scalar(eps_sb[:], eps_sb[:], 1.0, None, OP.mult, OP.bypass))

        def ln_prep(src):
            for c in range(8):
                ins = v.bn_stats(stat6[:, 6 * c:6 * c + 6],
                                 src[:, 512 * c:512 * c + 512])
            inc("v", ins)
            wt(v, "v")
            inc("v", v.bn_aggr(mv2[:], stat6[:, 0:48]))
            wt(a, "v")
            inc("a", a.activation(std1[:], mv2[:, 1:2], AF.Sqrt,
                                  bias=eps_sb[:], scale=1.0))
            wt(v, "a")
            inc("v", v.reciprocal(r1[:], std1[:]))
            wt(v, "v")
            inc("v", v.scalar_tensor_tensor(nmr1[:], mv2[:, 0:1], -1.0, r1[:],
                                            OP.mult, OP.mult))
            wt(v, "v")

        tp_war = {}

        def transpose_tile(src, width, dst3):
            nh = width // 128
            for grp in range(nh // 4):
                b_ = 4 + grp % 2
                if tp_war.get(b_) is not None:
                    wt(t, *tp_war[b_])
                for k in range(4):
                    hh = grp * 4 + k
                    ins = t.matmul(bk(b_, 128 * k, 128),
                                   src[:, 128 * hh:128 * hh + 128], ident[:],
                                   start=True, stop=True)
                inc("t", ins)
                eng, s_ = (v, "v") if grp % 2 == 0 else (a, "a")
                wt(eng, "t")
                pv = bk(b_).rearrange("p (k m) -> p k m", k=4)
                if eng is v:
                    ins = v.tensor_copy(dst3(grp * 4, 4), pv)
                else:
                    ins = a.activation(dst3(grp * 4, 4), pv, AF.Copy)
                tp_war[b_] = (s_, inc(s_, ins))

        xnT_m = salloc("xnT_m", [128, HC * TOK], BF16, side="right")
        xnT_h = salloc("xnT_h", [128, HC * HALO], BF16, side="right")
        xb = [salloc("xbA0", [128, D], F32, side="right"),
              salloc("xbA1", [128, D], F32, side="right")]
        xt = [salloc("xtA0", [128, D], BF16, side="right"),
              salloc("xtA1", [128, D], BF16, side="right")]
        tiles = [("m", i) for i in range(8)] + [("h", i) for i in range(4)]
        ld_at = {}
        xb_war = [0, 0]
        xt_war = [0, 0]

        def loadA(i):
            kind, tc = tiles[i]
            src = xm if kind == "m" else xh
            wt(sy, "v", xb_war[i % 2])
            ld_at[i] = ddma(f"xa{i % 2}", sy, xb[i % 2][:],
                            src[tc * 128:tc * 128 + 128, :])

        loadA(0)
        loadA(1)
        dm = xnT_m.rearrange("p (h w) -> p h w", h=HC)
        dh = xnT_h.rearrange("p (h w) -> p h w", h=HC)
        for i, (kind, tc) in enumerate(tiles):
            dwt(v, ld_at[i])
            ln_prep(xb[i % 2])
            wt(v, "t", xt_war[i % 2])
            ins = v.tensor_scalar(xt[i % 2][:], xb[i % 2][:], r1[:], nmr1[:],
                                  OP.mult, OP.add)
            xb_war[i % 2] = inc("v", ins)
            wt(t, "v")
            dst = dm if kind == "m" else dh

            def dst3(h0, n, tc=tc, dst=dst):
                return dst[:, h0:h0 + n, tc * 128:tc * 128 + 128]
            transpose_tile(xt[i % 2], D, dst3)
            xt_war[i % 2] = N["t"]
            if i + 2 < len(tiles):
                loadA(i + 2)
        sfree("xtA1", "xtA0", "xbA1", "xbA0")
        fence()

        K_sb = salloc("K", [128, NKV * KEYS], BF16)
        VT_sb = salloc("VT", [128, KC * NKV * 128], BF16)

        def proj_rhs(hh, tb):
            if tb == 0:
                return xnT_h[:, hh * HALO: hh * HALO + 512]
            o = hh * TOK + (tb - 1) * 512
            return xnT_m[:, o:o + 512]

        def do_proj(units, wbuf, ntb, store, pfx):
            wb_war = [0, 0]
            wd_at = {}
            pj_war = {}

            def loadW(idx):
                src4, fo = units[idx]
                wt(g, "t", wb_war[idx % 2])
                wd_at[idx] = ddma(f"{pfx}{idx % 2}", g, wbuf[idx % 2][:],
                                  src4[fo].rearrange("p h b -> p (h b)"))

            loadW(0)
            if len(units) > 1:
                loadW(1)
            for idx in range(len(units)):
                dwt(t, wd_at[idx])
                for tb in range(ntb):
                    b_ = tb % 2
                    if pj_war.get(b_) is not None:
                        wt(t, *pj_war[b_])
                    for hh in range(HC):
                        ins = t.matmul(bk(b_),
                                       wbuf[idx % 2][:, hh * 128:hh * 128 + 128],
                                       proj_rhs(hh, tb + (3 - ntb)),
                                       start=(hh == 0), stop=(hh == HC - 1))
                    inc("t", ins)
                    eng, s_ = (v, "v") if tb % 2 == 0 else (a, "a")
                    wt(eng, "t")
                    dstap = store(idx, tb)
                    if eng is v:
                        ins = v.tensor_copy(dstap, bk(b_))
                    else:
                        ins = a.activation(dstap, bk(b_), AF.Copy)
                    pj_war[b_] = (s_, inc(s_, ins))
                wb_war[idx % 2] = N["t"]
                if idx + 2 < len(units):
                    loadW(idx + 2)
                yield idx, pj_war

        wbB = [salloc("wbB0", [128, D], BF16), salloc("wbB1", [128, D], BF16)]
        vfm = salloc("vfm", [128, KEYS], BF16)
        unitsB1 = [(wk4, i) for i in range(NKV)] + [(wv4, i) for i in range(NKV)]

        def storeB1(idx, tb):
            if idx < NKV:
                o = idx * KEYS + tb * 512
                return K_sb[:, o:o + 512]
            return vfm[:, tb * 512: tb * 512 + 512]

        dVT = VT_sb.rearrange("p (c e m) -> p c e m", c=KC, e=NKV)
        for idx, pj_war in do_proj(unitsB1, wbB, 3, storeB1, "wb"):
            if idx >= NKV:
                fo = idx - NKV
                for b_ in (0, 1):
                    if pj_war.get(b_) is not None:
                        wt(t, *pj_war[b_])

                def dst3(tc0, n, fo=fo):
                    return dVT[:, tc0:tc0 + n, fo, :]
                transpose_tile(vfm, KEYS, dst3)
        sfree("vfm", "wbB1", "wbB0")
        sfree("xnT_h")
        fence()

        Q_sb = salloc("Q", [128, NQ * TOK], BF16)
        wbQ = [salloc("wbQ0", [128, D], BF16), salloc("wbQ1", [128, D], BF16)]
        unitsB2 = [(wq4, i) for i in range(NQ)]

        def storeB2(idx, tb):
            o = idx * TOK + tb * 512
            return Q_sb[:, o:o + 512]

        for idx, pj_war in do_proj(unitsB2, wbQ, 2, storeB2, "wb"):
            pass
        sfree("wbQ1", "wbQ0")
        sfree("xnT_m")
        fence()

        att_sb = salloc("att", [128, NQ * TOK], BF16, side="right")
        masks = salloc("masks", [128, 5 * 640], BF16)
        junk = salloc("junk", [128, 640], BF16)
        pexp = [salloc("pexp0", [128, 640], BF16), salloc("pexp1", [128, 640], BF16)]
        pms = [salloc("pm0", [128, 640], BF16), salloc("pm1", [128, 640], BF16)]
        pts = [salloc("pt0", [128, 640], BF16), salloc("pt1", [128, 640], BF16)]
        bias = [salloc("bias0", [128, 1], F32), salloc("bias1", [128, 1], F32)]
        rsum = [salloc("rs0", [128, 1], F32), salloc("rs1", [128, 1], F32)]
        rnrm = [salloc("rn0", [128, 1], F32), salloc("rn1", [128, 1], F32)]
        dwt(sy, cst_tag)
        cst_tag2 = ddma("cst", sy, masks.rearrange("p (v m) -> p v m", v=5),
                        maskd.rearrange("v p m -> p v m"))
        dwt(v, cst_tag2)
        mask3 = masks.rearrange("p (m w) -> p m w", m=5)

        last_ptevac = [None, None]
        last_attevac = None
        u = 0
        for qh in range(NQ):
            kvh = qh // 4
            for qc in range(8):
                par = u % 2
                so = 1024 * par
                if last_ptevac[par] is not None:
                    wt(t, *last_ptevac[par])
                lhs = Q_sb[:, qh * TOK + qc * 128: qh * TOK + qc * 128 + 128]
                kb = kvh * KEYS + qc * 128
                t.matmul(psum[:, so:so + 512], lhs, K_sb[:, kb:kb + 512],
                         start=True, stop=True)
                ins = t.matmul(psum[:, so + 512:so + 640], lhs,
                               K_sb[:, kb + 512:kb + 640], start=True, stop=True)
                inc("t", ins)
                wt(v, "t")
                ins = v.tensor_scalar(junk[:], psum[:, so:so + 640], -SCALE, None,
                                      OP.mult, OP.min, accum_out=bias[par][:])
                inc("v", ins)
                wt(a, "v")
                ins = a.activation(pexp[par][:], psum[:, so:so + 640], AF.Exp,
                                   bias=bias[par][:], scale=SCALE)
                inc("a", ins)
                mvv = min(qc, 4)
                wt(v, "a")
                ins = v.scalar_tensor_tensor(pms[par][:], pexp[par][:], 1.0,
                                             mask3[:, mvv, :], OP.mult, OP.mult,
                                             accum_out=rsum[par][:])
                inc("v", ins)
                wt(v, "v")
                inc("v", v.reciprocal(rnrm[par][:], rsum[par][:]))
                wt(v, "v")
                ins = v.tensor_scalar(pexp[par][:], pms[par][:], rnrm[par][:],
                                      None, OP.mult, OP.bypass)
                inc("v", ins)
                wt(t, "v")
                for j in range(5):
                    ins = t.matmul(psum[:, so + 128 * j: so + 128 * j + 128],
                                   pexp[par][:, j * 128:j * 128 + 128], ident[:],
                                   start=True, stop=True)
                inc("t", ins)
                wt(v, "t")
                ins = v.tensor_copy(pts[par][:], psum[:, so:so + 640])
                last_ptevac[par] = ("v", inc("v", ins))
                wt(t, *last_ptevac[par])
                if u % 4 == 0 and last_attevac is not None:
                    wt(t, *last_attevac)
                for j in range(5):
                    tc = qc + j
                    ins = t.matmul(bk(4, (u % 4) * 128, 128),
                                   VT_sb[:, (tc * NKV + kvh) * 128:
                                         (tc * NKV + kvh) * 128 + 128],
                                   pts[par][:, j * 128:j * 128 + 128],
                                   start=(j == 0), stop=(j == 4))
                inc("t", ins)
                if u % 4 == 3:
                    wt(a, "t")
                    dst = att_sb[:, qh * TOK + (qc - 3) * 128:
                                 qh * TOK + (qc - 3) * 128 + 512]
                    ins = a.activation(dst, bk(4), AF.Copy)
                    last_attevac = ("a", inc("a", ins))
                u += 1
        sfree("rn1", "rn0", "rs1", "rs0", "bias1", "bias0", "pt1", "pt0",
              "pm1", "pm0", "pexp1", "pexp0", "junk", "masks")
        sfree("Q", "VT", "K")
        fence()

        wo_out = salloc("wo_out", [128, 8 * D], BF16)
        sqd = salloc("sqd", [128, 512], BF16)
        wors = [salloc(f"wor{i}", [128, 16 * 512], BF16) for i in range(3)]
        wor_war = [0, 0, 0]
        wd_half = {}

        def load_wo(hi):
            fg, hf = divmod(hi, 2)
            ring = hi % 3
            wt(g, "t", wor_war[ring])
            wd_half[hi] = ddma(
                f"wo{ring}", g, wors[ring].rearrange("p (h b) -> p h b", h=16),
                wo4[fg][16 * hf:16 * hf + 16].rearrange("h p b -> p h b"))

        load_wo(0)
        load_wo(1)
        wo_bank_war = {}
        for fg in range(8):
            if 2 * fg + 2 < 16:
                load_wo(2 * fg + 2)
            dwt(t, wd_half[2 * fg])
            dwt(t, wd_half[2 * fg + 1])
            for tc in range(8):
                b_ = tc % 2
                if wo_bank_war.get(b_) is not None:
                    wt(t, *wo_bank_war[b_])
                for ac in range(32):
                    ring = (2 * fg + ac // 16) % 3
                    ins = t.matmul(
                        bk(b_),
                        att_sb[:, ac * TOK + tc * 128: ac * TOK + tc * 128 + 128],
                        wors[ring][:, (ac % 16) * 512:(ac % 16) * 512 + 512],
                        start=(ac == 0), stop=(ac == 31))
                inc("t", ins)
                wt(a, "t")
                dst = wo_out[:, tc * D + fg * 512: tc * D + fg * 512 + 512]
                inc("a", a.activation(dst, bk(b_), AF.Copy,
                                      accum_out=wos1[:, tc * 8 + fg: tc * 8 + fg + 1]))
                ins = a.activation(sqd[:], bk(b_), AF.Square,
                                   accum_out=wos2[:, tc * 8 + fg: tc * 8 + fg + 1])
                wo_bank_war[b_] = ("a", inc("a", ins))
            wor_war[(2 * fg) % 3] = N["t"]
            wor_war[(2 * fg + 1) % 3] = N["t"]
            if 2 * fg + 3 < 16:
                load_wo(2 * fg + 3)
        inc("a", a.activation(sqd[:, 0:1], eps_sb[:], AF.Copy))
        sfree("wor2", "wor1", "wor0", "sqd")
        sfree("att")
        fence()

        x1nT = salloc("x1nT", [128, HC * TOK], BF16, side="right")
        xbD = [salloc("xbD0", [128, D], F32), salloc("xbD1", [128, D], F32)]
        x1n0 = salloc("x1nD", [128, D], BF16)
        wt(v, "a")
        for tc in range(8):
            inc("v", v.tensor_reduce(sum1[:], wos1[:, tc * 8:tc * 8 + 8],
                                     axis=mybir.AxisListType.X, op=OP.add))
            inc("v", v.tensor_reduce(sum2[:], wos2[:, tc * 8:tc * 8 + 8],
                                     axis=mybir.AxisListType.X, op=OP.add))
            wt(v, "v")
            inc("v", v.tensor_scalar(mu1[:], sum1[:], 1.0 / D, None,
                                     OP.mult, OP.bypass))
            wt(v, "v")
            inc("v", v.scalar_tensor_tensor(std1[:], mu1[:], 1.0, mu1[:],
                                            OP.mult, OP.mult))
            wt(v, "v")
            inc("v", v.tensor_scalar(mv2[:, 1:2], sum2[:], 1.0 / D, std1[:],
                                     OP.mult, OP.subtract))
            wt(a, "v")
            inc("a", a.activation(std1[:], mv2[:, 1:2], AF.Sqrt,
                                  bias=eps_sb[:], scale=1.0))
            wt(v, "a")
            inc("v", v.reciprocal(r2a[:, tc:tc + 1], std1[:]))
            wt(v, "v")
            inc("v", v.scalar_tensor_tensor(nmr2a[:, tc:tc + 1], mu1[:], -1.0,
                                            r2a[:, tc:tc + 1], OP.mult, OP.mult))
            wt(v, "v")

        ldD = {}
        xbD_war = [0, 0]
        stD = {}
        x1st_at = []

        def loadD(tc):
            wt(sy, "v", xbD_war[tc % 2])
            dwt(sy, stD.get(tc % 2))
            ldD[tc] = ddma(f"xd{tc % 2}", sy, xbD[tc % 2][:],
                           xm[tc * 128:tc * 128 + 128, :])

        loadD(0)
        loadD(1)
        dx1 = x1nT.rearrange("p (h w) -> p h w", h=HC)
        for tc in range(8):
            x_ = xbD[tc % 2]
            dwt(v, ldD[tc])
            inc("v", v.tensor_scalar(x_[:], x_[:], nmr2a[:, tc:tc + 1], None,
                                     OP.add, OP.bypass))
            ins = v.scalar_tensor_tensor(x_[:], wo_out[:, tc * D:(tc + 1) * D],
                                         r2a[:, tc:tc + 1], x_[:], OP.mult, OP.add)
            inc("v", ins)
            wt(g, "v")
            stD[tc % 2] = ddma(f"xs{tc % 2}", g,
                               x1d[tc * 128:tc * 128 + 128, :], x_[:])
            x1st_at.append(stD[tc % 2])
            ln_prep(x_)
            inc("v", v.tensor_scalar(x1n0[:], x_[:], r1[:], nmr1[:],
                                     OP.mult, OP.add))
            wt(t, "v")

            def dst3(h0, n, tc=tc):
                return dx1[:, h0:h0 + n, tc * 128:tc * 128 + 128]
            transpose_tile(x1n0, D, dst3)
            xbD_war[tc % 2] = N["v"]
            wt(v, "t")
            if tc + 2 < 8:
                loadD(tc + 2)
        sfree("x1nD", "xbD1", "xbD0", "wo_out")
        fence()

        h_sb = salloc("h_sb", [128, IC * 512], BF16)
        qb = {"G": [salloc("qgb0", [128, 1024], I32), salloc("qgb1", [128, 1024], I32)],
              "U": [salloc("qub0", [128, 1024], I32), salloc("qub1", [128, 1024], I32)]}
        szb = [salloc("szb0", [128, 128], F32), salloc("szb1", [128, 128], F32)]
        dqs = [salloc("dqE0", [128, 1024], BF16), salloc("dqE1", [128, 1024], BF16)]
        wgts = [salloc("wgt0", [128, 1024], BF16), salloc("wgt1", [128, 1024], BF16)]
        sgb = [salloc("sg0", [128, 512], BF16), salloc("sg1", [128, 512], BF16)]
        hsp = [salloc(f"hsp{i}", [128, 512], BF16) for i in range(4)]

        deq_eng = []
        for k in range(16):
            deq_eng.append("v" if k < 9 else ("a" if k < 13 else "p"))
        deq_idx = 0

        gran = [(ic, m, gr) for ic in range(IC) for m in ("G", "U")
                for gr in range(4)]
        ldE = {}
        qE_war = {}
        szE_war = [0, 0]

        def loadE_sz(ic):
            wt(sy, "t", szE_war[ic % 2])
            ldE[("sz", ic)] = ddma(
                f"sz{ic % 2}", sy, szb[ic % 2].rearrange("p (k m) -> p k m", k=4),
                szgu[ic].rearrange("k p m -> p k m"))

        def loadE(gi):
            ic, m, gr = gran[gi]
            wt(sy, "t", qE_war.get((m, gr % 2), 0))
            src = qg5 if m == "G" else qu5
            ldE[gi] = ddma(f"q{m}{gr % 2}", sy, qb[m][gr % 2][:],
                           src[ic][:, gr * 1024:gr * 1024 + 1024])

        PRE = 2
        loadE_sz(0)
        loadE_sz(1)
        for j in range(PRE):
            loadE(j)

        negz_at = {}
        nzs_at = {}
        sg_war = [0, 0]
        dq_war = [0, 0]
        mm_bank_war = {}
        spill_at = {}
        gi = 0
        for ic in range(IC):
            for m in ("G", "U"):
                moff = 0 if m == "G" else 64
                scol = szb[ic % 2][:, (0 if m == "G" else 64):
                                   (0 if m == "G" else 64) + 32]
                zcol = szb[ic % 2][:, (32 if m == "G" else 96):
                                   (32 if m == "G" else 96) + 32]
                dwt(v, ldE[("sz", ic)])
                inc("v", v.tensor_scalar(zcol, zcol, -1.0, None,
                                         OP.mult, OP.bypass))
                negz_at[(ic, m)] = N["v"]
                wt(v, "v")
                wt(v, "a")
                ins = v.scalar_tensor_tensor(nzs[:, (0 if m == "G" else 32):
                                             (0 if m == "G" else 32) + 32],
                                             zcol, 1.0, scol, OP.mult, OP.mult)
                nzs_at[(ic, m)] = inc("v", ins)
                for gr in range(4):
                    gi += 1
                    qbuf = qb[m][gr % 2]
                    dq = dqs[gr % 2]
                    wgt = wgts[gr % 2]
                    ldtag = ldE[(ic, m, gr) if False else gi - 1]
                    done = {}
                    for k in range(8):
                        hh = gr * 8 + k
                        eng = deq_eng[deq_idx % 16]
                        deq_idx += 1
                        sl = slice(k * 128, k * 128 + 128)
                        zc = zcol[:, hh - gr * 8 + gr * 8: hh + 1][:, 0:1] \
                            if False else zcol[:, hh:hh + 1]
                        sc = scol[:, hh:hh + 1]
                        if eng == "v":
                            dwt(v, ldtag)
                            wt(v, "v", negz_at[(ic, m)])
                            wt(v, "t", dq_war[gr % 2])
                            ins = v.tensor_scalar(dq[:, sl], qbuf[:, sl], zc, sc,
                                                  OP.add, OP.mult)
                            done["v"] = inc("v", ins)
                        elif eng == "a":
                            dwt(a, ldtag)
                            wt(a, "v", nzs_at[(ic, m)])
                            wt(a, "t", dq_war[gr % 2])
                            nb = nzs[:, (0 if m == "G" else 32) + hh:
                                     (0 if m == "G" else 32) + hh + 1]
                            ins = a.activation(dq[:, sl], qbuf[:, sl], AF.Identity,
                                               scale=sc, bias=nb)
                            done["a"] = inc("a", ins)
                        else:
                            dwt(g, ldtag)
                            wt(g, "v", negz_at[(ic, m)])
                            wt(g, "t", dq_war[gr % 2])
                            ins = g.tensor_scalar(dq[:, sl], qbuf[:, sl], zc, sc,
                                                  OP.add, OP.mult)
                            done["p"] = inc("p", ins)
                    wready = []
                    for half in range(2):
                        b_ = 4 + half
                        if tp_war.get(b_) is not None:
                            wt(t, *tp_war[b_])
                        for s_, cnt in done.items():
                            wt(t, s_, cnt)
                        for k in range(4):
                            kk = half * 4 + k
                            ins = t.matmul(bk(b_, 128 * k, 128),
                                           dq[:, kk * 128:kk * 128 + 128],
                                           ident[:], start=True, stop=True)
                        inc("t", ins)
                        eng2, s2 = (v, "v") if half == 0 else (a, "a")
                        wt(eng2, "t")
                        dst = wgt[:, half * 512:half * 512 + 512]
                        if eng2 is v:
                            ins = v.tensor_copy(dst, bk(b_))
                        else:
                            ins = a.activation(dst, bk(b_), AF.Copy)
                        cnt2 = inc(s2, ins)
                        tp_war[b_] = (s2, cnt2)
                        wready.append((s2, cnt2))
                    dq_war[gr % 2] = N["t"]
                    bb = 0 if m == "G" else 2
                    for s2, cnt2 in wready:
                        wt(t, s2, cnt2)
                    for tb in range(2):
                        b_ = bb + tb
                        if gr == 0 and mm_bank_war.get(b_) is not None:
                            wt(t, *mm_bank_war[b_])
                        for k in range(8):
                            hh = gr * 8 + k
                            ins = t.matmul(
                                bk(b_), wgt[:, k * 128:k * 128 + 128],
                                x1nT[:, hh * TOK + tb * 512: hh * TOK + tb * 512 + 512],
                                start=(hh == 0), stop=(hh == HC - 1))
                        inc("t", ins)
                    qE_war[(m, gr % 2)] = N["t"]
                    if gi - 1 + PRE < len(gran):
                        loadE(gi - 1 + PRE)
                if m == "U":
                    for tb in range(2):
                        wt(a, "t")
                        if ic >= 1:
                            wt(a, "v", sg_war[tb])
                        inc("a", a.activation(sgb[tb][:], bk(0 + tb), AF.Silu))
                        wt(v, "a")
                        if tb == 0:
                            hdst = h_sb[:, ic * 512:(ic + 1) * 512]
                        else:
                            dwt(v, spill_at.get(ic % 4))
                            hdst = hsp[ic % 4][:]
                        ins = v.tensor_tensor(hdst, sgb[tb][:], bk(2 + tb), OP.mult)
                        cnt = inc("v", ins)
                        sg_war[tb] = cnt
                        if tb == 1:
                            wt(g, "v", cnt)
                            spill_at[ic % 4] = ddma(f"hs{ic % 4}", g,
                                                    hbd[:, ic, :], hsp[ic % 4][:])
                    for b_ in range(4):
                        mm_bank_war[b_] = ("v", N["v"])
            szE_war[ic % 2] = N["t"]
            if ic + 2 < IC:
                loadE_sz(ic + 2)
        spill_done = [spill_at[k] for k in spill_at]
        sfree("hsp3", "hsp2", "hsp1", "hsp0", "sg1", "sg0", "wgt1", "wgt0",
              "dqE1", "dqE0", "szb1", "szb0", "qub1", "qub0", "qgb1", "qgb0")
        sfree("x1nT")
        fence()

        ffn_sb = salloc("ffn", [128, 32 * TOK], BF16, side="right")
        qdb = [[salloc("qdb00", [128, 1024], I32), salloc("qdb01", [128, 1024], I32)],
               [salloc("qdb10", [128, 1024], I32), salloc("qdb11", [128, 1024], I32)]]
        hbb = [salloc("hbb0", [128, 4096], BF16), salloc("hbb1", [128, 4096], BF16)]
        szdb = [salloc("szdb0", [128, 344], F32), salloc("szdb1", [128, 344], F32)]
        dqd = [salloc("dqd0", [128, 256], BF16), salloc("dqd1", [128, 256], BF16)]
        wdt = [salloc("wdt0", [128, 256], BF16), salloc("wdt1", [128, 256], BF16)]

        nblk = 11
        ldF = {}
        qF_war = {}
        hF_war = {}
        szF_war = [0, 0]

        def loadF_sz(grp):
            wt(sy, "t", szF_war[grp % 2])
            ldF[("sz", grp)] = ddma(
                f"szd{grp % 2}", sy,
                szdb[grp % 2].rearrange("p (a b m) -> p a b m", a=2, b=2),
                szd[grp].rearrange("a b p m -> p a b m"))

        def loadF(grp, bi):
            ic0 = bi * 8
            nic = min(8, IC - ic0)
            par = bi % 2
            tags = []
            for fsub in range(2):
                fc = grp * 2 + fsub
                wt(sy, "t", qF_war.get((fsub, par), 0))
                tags.append(ddma(
                    f"qd{fsub}{par}", sy, qdb[fsub][par][:, 0:nic * 128],
                    qd5[fc][:, ic0 * 128: ic0 * 128 + nic * 128]))
            wt(sy, "t", hF_war.get(par, 0))
            for sp in spill_done:
                dwt(sy, sp)
            tags.append(ddma(
                f"hb{par}", sy, hbb[par][:, 0:nic * 512],
                hbd[:, ic0:ic0 + nic, :].rearrange("p i m -> p (i m)")))
            ldF[(grp, bi)] = tags

        loadF_sz(0)
        loadF(0, 0)
        dqF_war = [0, 0]
        tpd_war = {}
        dn_bank_war = {}
        negzF = {}
        for grp in range(16):
            if grp + 1 < 16:
                loadF_sz(grp + 1)
            dwt(v, ldF[("sz", grp)])
            inc("v", v.tensor_scalar(szdb[grp % 2][:, 86:172],
                                     szdb[grp % 2][:, 86:172], -1.0, None,
                                     OP.mult, OP.bypass))
            ins = v.tensor_scalar(szdb[grp % 2][:, 258:344],
                                  szdb[grp % 2][:, 258:344], -1.0, None,
                                  OP.mult, OP.bypass)
            negzF[grp] = inc("v", ins)
            for bi in range(nblk):
                if bi + 1 < nblk:
                    loadF(grp, bi + 1)
                par = bi % 2
                ic0 = bi * 8
                nic = min(8, IC - ic0)
                ldtag = ldF[(grp, bi)]
                for ii in range(nic):
                    ic = ic0 + ii
                    dq = dqd[ic % 2]
                    wd_ = wdt[ic % 2]
                    done = {}
                    for fsub in range(2):
                        eng = deq_eng[deq_idx % 16]
                        deq_idx += 1
                        if eng == "a":
                            eng = "v"
                        zc = szdb[grp % 2][:, fsub * 172 + 86 + ic:
                                           fsub * 172 + 86 + ic + 1]
                        sc = szdb[grp % 2][:, fsub * 172 + ic: fsub * 172 + ic + 1]
                        qsl = qdb[fsub][par][:, ii * 128: ii * 128 + 128]
                        dsl = dq[:, fsub * 128:fsub * 128 + 128]
                        if eng == "v":
                            for tg in ldtag:
                                dwt(v, tg)
                            wt(v, "v", negzF[grp])
                            wt(v, "t", dqF_war[ic % 2])
                            ins = v.tensor_scalar(dsl, qsl, zc, sc, OP.add, OP.mult)
                            done["v"] = inc("v", ins)
                        else:
                            for tg in ldtag:
                                dwt(g, tg)
                            wt(g, "v", negzF[grp])
                            wt(g, "t", dqF_war[ic % 2])
                            ins = g.tensor_scalar(dsl, qsl, zc, sc, OP.add, OP.mult)
                            done["p"] = inc("p", ins)
                    b_ = 4 + (ic % 2)
                    if tpd_war.get(b_) is not None:
                        wt(t, *tpd_war[b_])
                    for s_, cnt in done.items():
                        wt(t, s_, cnt)
                    for fsub in range(2):
                        ins = t.matmul(bk(b_, fsub * 128, 128),
                                       dq[:, fsub * 128:fsub * 128 + 128],
                                       ident[:], start=True, stop=True)
                    dqF_war[ic % 2] = inc("t", ins)
                    eng2, s2 = (v, "v") if ic % 2 == 0 else (a, "a")
                    wt(eng2, "t")
                    if eng2 is v:
                        ins = v.tensor_copy(wd_[:], bk(b_, 0, 256))
                    else:
                        ins = a.activation(wd_[:], bk(b_, 0, 256), AF.Copy)
                    tpd_war[b_] = (s2, inc(s2, ins))
                    wt(t, s2, tpd_war[b_][1])
                    for fsub in range(2):
                        for th in range(2):
                            b2 = fsub * 2 + th
                            if ic == 0 and dn_bank_war.get(b2) is not None:
                                wt(t, *dn_bank_war[b2])
                            rhs = h_sb[:, ic * 512:(ic + 1) * 512] if th == 0 \
                                else hbb[par][:, ii * 512: ii * 512 + 512]
                            ins = t.matmul(bk(b2),
                                           wd_[:, fsub * 128:fsub * 128 + 128],
                                           rhs, start=(ic == 0), stop=(ic == IC - 1))
                    inc("t", ins)
                for fsub in range(2):
                    qF_war[(fsub, par)] = N["t"]
                hF_war[par] = N["t"]
                if bi == nblk - 1 and grp + 1 < 16:
                    loadF(grp + 1, 0)
            szF_war[grp % 2] = N["t"]
            for fsub in range(2):
                for th in range(2):
                    b2 = fsub * 2 + th
                    fck = grp * 2 + fsub
                    wt(a, "t")
                    dst = ffn_sb[:, fck * TOK + th * 512: fck * TOK + th * 512 + 512]
                    ins = a.activation(dst, bk(b2), AF.Copy)
                    dn_bank_war[b2] = ("a", inc("a", ins))
        sfree("wdt1", "wdt0", "dqd1", "dqd0", "szdb1", "szdb0",
              "hbb1", "hbb0", "qdb11", "qdb10", "qdb01", "qdb00", "h_sb")
        fence()

        fT0 = salloc("fT0", [128, D], BF16)
        x1t = [salloc("x1t0", [128, D], F32), salloc("x1t1", [128, D], F32)]
        ldG = {}
        x1t_war = [0, 0]
        stG = {}

        def loadG(tc):
            wt(sy, "v", x1t_war[tc % 2])
            dwt(sy, stG.get(tc % 2))
            dwt(sy, x1st_at[tc])
            ldG[tc] = ddma(f"xg{tc % 2}", sy, x1t[tc % 2][:],
                           x1d[tc * 128:tc * 128 + 128, :])

        loadG(0)
        loadG(1)
        ffn3 = ffn_sb.rearrange("p (f w) -> p f w", f=32)
        fdst = fT0.rearrange("p (f w) -> p f w", f=32)
        fT_war = 0
        for tc in range(8):
            for grp in range(8):
                b_ = 4 + grp % 2
                if tp_war.get(b_) is not None:
                    wt(t, *tp_war[b_])
                wt(t, "v", fT_war)
                for k in range(4):
                    f = grp * 4 + k
                    ins = t.matmul(bk(b_, 128 * k, 128),
                                   ffn3[:, f, tc * 128:tc * 128 + 128],
                                   ident[:], start=True, stop=True)
                inc("t", ins)
                eng, s_ = (v, "v") if grp % 2 == 0 else (a, "a")
                wt(eng, "t")
                pv = bk(b_).rearrange("p (k m) -> p k m", k=4)
                if eng is v:
                    ins = v.tensor_copy(fdst[:, grp * 4:grp * 4 + 4, :], pv)
                else:
                    ins = a.activation(fdst[:, grp * 4:grp * 4 + 4, :], pv, AF.Copy)
                tp_war[b_] = (s_, inc(s_, ins))
            wt(v, "a")
            wt(v, "v")
            ln_prep(fT0)
            dwt(v, ldG[tc])
            x_ = x1t[tc % 2]
            inc("v", v.tensor_scalar(x_[:], x_[:], nmr1[:], None, OP.add, OP.bypass))
            ins = v.scalar_tensor_tensor(x_[:], fT0[:], r1[:], x_[:],
                                         OP.mult, OP.add)
            fT_war = inc("v", ins)
            x1t_war[tc % 2] = N["v"]
            wt(g, "v")
            stG[tc % 2] = ddma(f"os{tc % 2}", g,
                               outd[tc * 128:tc * 128 + 128, :], x_[:])
            if tc + 2 < 8:
                loadG(tc + 2)
        sfree("x1t1", "x1t0", "fT0")
        for nm in sorted(dslots):
            sy.wait_ge(dslots[nm][0], dslots[nm][1])

    @blk.sync
    def _(_sync):
        emit()

    st.close()
    return nc


def _prep_inputs(inp):
    f32 = np.float32
    x = np.asarray(inp["L0_attn_x"], f32)
    wq = np.asarray(inp["L0_attn_wq"], f32)
    wk = np.asarray(inp["L0_attn_wk"], f32)
    wv = np.asarray(inp["L0_attn_wv"], f32)
    wo = np.asarray(inp["L0_attn_wo"], f32)
    qg = np.asarray(inp["L0_ffn_qw_gate"], np.int32)
    qu = np.asarray(inp["L0_ffn_qw_up"], np.int32)
    qd = np.asarray(inp["L0_ffn_qw_down"], np.int32)
    sg = np.asarray(inp["L0_ffn_s_gate"], f32)
    zg = np.asarray(inp["L0_ffn_z_gate"], f32)
    su = np.asarray(inp["L0_ffn_s_up"], f32)
    zu = np.asarray(inp["L0_ffn_z_up"], f32)
    sd = np.asarray(inp["L0_ffn_s_down"], f32)
    zd = np.asarray(inp["L0_ffn_z_down"], f32)

    def tile4(wT, nf):
        return np.ascontiguousarray(
            wT.reshape(HC, 128, nf, 128).transpose(2, 1, 0, 3))

    wq4 = tile4(wq.T, NQ)
    wk4 = tile4(wk.T, NKV)
    wv4 = tile4(wv.T, NKV)
    wo4 = np.ascontiguousarray(wo.T.reshape(32, 128, 8, 512).transpose(2, 0, 1, 3))
    qg5 = np.ascontiguousarray(qg.reshape(IC, 128, D))
    qu5 = np.ascontiguousarray(qu.reshape(IC, 128, D))
    qd5 = np.ascontiguousarray(qd.reshape(32, 128, II))
    szgu = np.ascontiguousarray(
        np.stack([sg.reshape(IC, 128, 32), zg.reshape(IC, 128, 32),
                  su.reshape(IC, 128, 32), zu.reshape(IC, 128, 32)], axis=1))
    sd5 = sd.reshape(16, 2, 128, IC)
    zd5 = zd.reshape(16, 2, 128, IC)
    szd_ = np.ascontiguousarray(np.stack([sd5, zd5], axis=2))

    ident = np.eye(128, dtype=ml_dtypes.bfloat16)
    jj = np.arange(640)[None, :]
    pp = np.arange(128)[:, None]
    base = (jj > pp) & (jj <= pp + 512)
    m_even = np.stack([(base & (jj >= 512 - vv * 128)) if vv < 4 else base
                       for vv in range(5)]).astype(ml_dtypes.bfloat16)
    m_odd = np.stack([base] * 5).astype(ml_dtypes.bfloat16)

    in_maps = []
    for core in range(8):
        b, half = divmod(core, 2)
        q0 = half * TOK
        xmain = np.ascontiguousarray(x[b, q0:q0 + TOK])
        xhalo = (np.ascontiguousarray(x[b, q0 - HALO:q0]) if half == 1
                 else np.zeros((HALO, D), f32))
        in_maps.append({
            "xm": xmain, "xh": xhalo,
            "wq4": wq4, "wk4": wk4, "wv4": wv4, "wo4": wo4,
            "qg5": qg5, "qu5": qu5, "qd5": qd5,
            "szgu": szgu, "szd": szd_,
            "maskd": m_odd if half == 1 else m_even,
            "identd": ident,
        })
    return in_maps


def kernel(**inputs):
    in_maps = _prep_inputs(inputs)
    nc = build_core_program()
    res = run_bass_kernel_spmd(nc, in_maps, core_ids=list(range(8)))
    out = np.empty((4, 2048, D), np.float32)
    for core in range(8):
        b, half = divmod(core, 2)
        out[b, half * TOK:(half + 1) * TOK] = res.results[core]["outd"]
    return out


# version 05
# speedup vs baseline: 1.0015x; 1.0015x over previous
import numpy as np
import ml_dtypes
import contextlib
import concourse.bass as bass
import concourse.mybir as mybir
from concourse.alu_op_type import AluOpType
from concourse.bass_utils import run_bass_kernel_spmd

F32 = mybir.dt.float32
BF16 = mybir.dt.bfloat16
I32 = mybir.dt.int32
AF = mybir.ActivationFunctionType
OP = AluOpType

TOK = 1024
HALO = 512
D = 4096
HC = 32
NQ, NKV, HD = 32, 8, 128
II = 11008
IC = 86
SCALE = float(1.0 / np.sqrt(128.0))
EPS = 1e-5
KEYS = TOK + HALO
KC = 12


def build_core_program():
    nc = bass.Bass("TRN2", target_bir_lowering=False)

    xm = nc.dram_tensor("xm", [TOK, D], F32, kind="ExternalInput").ap()
    xh = nc.dram_tensor("xh", [HALO, D], F32, kind="ExternalInput").ap()
    wq4 = nc.dram_tensor("wq4", [NQ, 128, HC, 128], F32, kind="ExternalInput").ap()
    wk4 = nc.dram_tensor("wk4", [NKV, 128, HC, 128], F32, kind="ExternalInput").ap()
    wv4 = nc.dram_tensor("wv4", [NKV, 128, HC, 128], F32, kind="ExternalInput").ap()
    wo4 = nc.dram_tensor("wo4", [8, 32, 128, 512], F32, kind="ExternalInput").ap()
    qg5 = nc.dram_tensor("qg5", [IC, 128, D], I32, kind="ExternalInput").ap()
    qu5 = nc.dram_tensor("qu5", [IC, 128, D], I32, kind="ExternalInput").ap()
    qd5 = nc.dram_tensor("qd5", [32, 128, II], I32, kind="ExternalInput").ap()
    szgu = nc.dram_tensor("szgu", [IC, 4, 128, 32], F32, kind="ExternalInput").ap()
    szd = nc.dram_tensor("szd", [16, 2, 2, 128, IC], F32, kind="ExternalInput").ap()
    maskd = nc.dram_tensor("maskd", [5, 128, 640], BF16, kind="ExternalInput").ap()
    identd = nc.dram_tensor("identd", [128, 128], BF16, kind="ExternalInput").ap()
    outd = nc.dram_tensor("outd", [TOK, D], F32, kind="ExternalOutput").ap()
    x1d = nc.dram_tensor("x1d", [TOK, D], F32).ap()
    hbd = nc.dram_tensor("hbd", [128, IC, 512], BF16).ap()

    st = contextlib.ExitStack()

    SEMS = ["t", "v", "a", "p"]
    H = {s: st.enter_context(nc.semaphore(s)) for s in SEMS}
    N = {s: 0 for s in SEMS}

    def inc(s, instr, k=1):
        instr.then_inc(H[s], k)
        N[s] += k
        return N[s]

    def wt(eng, s, target=None):
        if target is None:
            target = N[s]
        if target > 0:
            eng.wait_ge(H[s], target)

    dslots = {}

    def ddma(name, qeng, dst, src_):
        if name not in dslots:
            dslots[name] = [st.enter_context(nc.semaphore("D" + name)), 0]
        ent = dslots[name]
        qeng.dma_start(dst, src_).then_inc(ent[0], 16)
        ent[1] += 16
        return (name, ent[1])

    def dwt(eng, tag):
        if tag is None:
            return
        nm, target = tag
        eng.wait_ge(dslots[nm][0], target)

    guards = {}

    def salloc(name, shape, dt, side="left"):
        gd = nc.sbuf_tensor(name, shape, dt, side=side)
        h = gd.__enter__()
        guards[name] = gd
        return h

    def sfree(*names):
        for name in names:
            guards.pop(name).__exit__(None, None, None)

    psum = st.enter_context(nc.psum_tensor("ps", [128, 4096], F32))

    def bk(k, off=0, n=512):
        return psum[:, 512 * k + off: 512 * k + off + n]

    ident = salloc("ident", [128, 128], BF16)
    eps_sb = salloc("eps", [128, 1], F32)
    stat6 = salloc("stat6", [128, 48], F32)
    mv2 = salloc("mv2", [128, 2], F32)
    std1 = salloc("std1", [128, 1], F32)
    r1 = salloc("r1", [128, 1], F32)
    nmr1 = salloc("nmr1", [128, 1], F32)
    wos1 = salloc("wos1", [128, 64], F32)
    wos2 = salloc("wos2", [128, 64], F32)
    r2a = salloc("r2a", [128, 8], F32)
    nmr2a = salloc("nmr2a", [128, 8], F32)
    sum1 = salloc("sum1", [128, 1], F32)
    sum2 = salloc("sum2", [128, 1], F32)
    mu1 = salloc("mu1", [128, 1], F32)
    nzs = salloc("nzs", [128, 64], F32)

    v, a, t, g, sy = nc.vector, nc.scalar, nc.tensor, nc.gpsimd, nc.sync

    blk = st.enter_context(nc.Block())

    def fence():
        for eng in (t, v, a, g, sy):
            for s in ("v", "a", "t", "p"):
                wt(eng, s)

    def emit():
        cst_tag = ddma("cst", sy, ident[:], identd)
        dwt(v, cst_tag)
        v.memset(eps_sb[:], EPS)
        inc("v", v.tensor_scalar(eps_sb[:], eps_sb[:], 1.0, None, OP.mult, OP.bypass))

        def ln_prep(src):
            for c in range(8):
                ins = v.bn_stats(stat6[:, 6 * c:6 * c + 6],
                                 src[:, 512 * c:512 * c + 512])
            inc("v", ins)
            wt(v, "v")
            inc("v", v.bn_aggr(mv2[:], stat6[:, 0:48]))
            wt(a, "v")
            inc("a", a.activation(std1[:], mv2[:, 1:2], AF.Sqrt,
                                  bias=eps_sb[:], scale=1.0))
            wt(v, "a")
            inc("v", v.reciprocal(r1[:], std1[:]))
            wt(v, "v")
            inc("v", v.scalar_tensor_tensor(nmr1[:], mv2[:, 0:1], -1.0, r1[:],
                                            OP.mult, OP.mult))
            wt(v, "v")

        tp_war = {}

        def transpose_tile(src, width, dst3):
            nh = width // 128
            for grp in range(nh // 4):
                b_ = 4 + grp % 2
                if tp_war.get(b_) is not None:
                    wt(t, *tp_war[b_])
                for k in range(4):
                    hh = grp * 4 + k
                    ins = t.matmul(bk(b_, 128 * k, 128),
                                   src[:, 128 * hh:128 * hh + 128], ident[:],
                                   start=True, stop=True)
                inc("t", ins)
                eng, s_ = (v, "v") if grp % 2 == 0 else (a, "a")
                wt(eng, "t")
                pv = bk(b_).rearrange("p (k m) -> p k m", k=4)
                if eng is v:
                    ins = v.tensor_copy(dst3(grp * 4, 4), pv)
                else:
                    ins = a.activation(dst3(grp * 4, 4), pv, AF.Copy)
                tp_war[b_] = (s_, inc(s_, ins))

        xnT_m = salloc("xnT_m", [128, HC * TOK], BF16, side="right")
        xnT_h = salloc("xnT_h", [128, HC * HALO], BF16, side="right")
        xb = [salloc("xbA0", [128, D], F32, side="right"),
              salloc("xbA1", [128, D], F32, side="right")]
        xt = [salloc("xtA0", [128, D], BF16, side="right"),
              salloc("xtA1", [128, D], BF16, side="right")]
        tiles = [("m", i) for i in range(8)] + [("h", i) for i in range(4)]
        ld_at = {}
        xb_war = [0, 0]
        xt_war = [0, 0]

        def loadA(i):
            kind, tc = tiles[i]
            src = xm if kind == "m" else xh
            wt(sy, "v", xb_war[i % 2])
            ld_at[i] = ddma(f"xa{i % 2}", sy, xb[i % 2][:],
                            src[tc * 128:tc * 128 + 128, :])

        loadA(0)
        loadA(1)
        dm = xnT_m.rearrange("p (h w) -> p h w", h=HC)
        dh = xnT_h.rearrange("p (h w) -> p h w", h=HC)
        for i, (kind, tc) in enumerate(tiles):
            dwt(v, ld_at[i])
            ln_prep(xb[i % 2])
            wt(v, "t", xt_war[i % 2])
            ins = v.tensor_scalar(xt[i % 2][:], xb[i % 2][:], r1[:], nmr1[:],
                                  OP.mult, OP.add)
            xb_war[i % 2] = inc("v", ins)
            wt(t, "v")
            dst = dm if kind == "m" else dh

            def dst3(h0, n, tc=tc, dst=dst):
                return dst[:, h0:h0 + n, tc * 128:tc * 128 + 128]
            transpose_tile(xt[i % 2], D, dst3)
            xt_war[i % 2] = N["t"]
            if i + 2 < len(tiles):
                loadA(i + 2)
        sfree("xtA1", "xtA0", "xbA1", "xbA0")
        fence()

        K_sb = salloc("K", [128, NKV * KEYS], BF16)
        VT_sb = salloc("VT", [128, KC * NKV * 128], BF16)

        def proj_rhs(hh, tb):
            if tb == 0:
                return xnT_h[:, hh * HALO: hh * HALO + 512]
            o = hh * TOK + (tb - 1) * 512
            return xnT_m[:, o:o + 512]

        def do_proj(units, wbuf, ntb, store, pfx):
            wb_war = [0, 0]
            wd_at = {}
            pj_war = {}

            def loadW(idx):
                src4, fo = units[idx]
                wt(g, "t", wb_war[idx % 2])
                wd_at[idx] = ddma(f"{pfx}{idx % 2}", g, wbuf[idx % 2][:],
                                  src4[fo].rearrange("p h b -> p (h b)"))

            loadW(0)
            if len(units) > 1:
                loadW(1)
            for idx in range(len(units)):
                dwt(t, wd_at[idx])
                for tb in range(ntb):
                    b_ = tb % 2
                    if pj_war.get(b_) is not None:
                        wt(t, *pj_war[b_])
                    for hh in range(HC):
                        ins = t.matmul(bk(b_),
                                       wbuf[idx % 2][:, hh * 128:hh * 128 + 128],
                                       proj_rhs(hh, tb + (3 - ntb)),
                                       start=(hh == 0), stop=(hh == HC - 1))
                    inc("t", ins)
                    eng, s_ = (v, "v") if tb % 2 == 0 else (a, "a")
                    wt(eng, "t")
                    dstap = store(idx, tb)
                    if eng is v:
                        ins = v.tensor_copy(dstap, bk(b_))
                    else:
                        ins = a.activation(dstap, bk(b_), AF.Copy)
                    pj_war[b_] = (s_, inc(s_, ins))
                wb_war[idx % 2] = N["t"]
                if idx + 2 < len(units):
                    loadW(idx + 2)
                yield idx, pj_war

        wbB = [salloc("wbB0", [128, D], BF16), salloc("wbB1", [128, D], BF16)]
        vfm = salloc("vfm", [128, KEYS], BF16)
        unitsB1 = [(wk4, i) for i in range(NKV)] + [(wv4, i) for i in range(NKV)]

        def storeB1(idx, tb):
            if idx < NKV:
                o = idx * KEYS + tb * 512
                return K_sb[:, o:o + 512]
            return vfm[:, tb * 512: tb * 512 + 512]

        dVT = VT_sb.rearrange("p (c e m) -> p c e m", c=KC, e=NKV)
        for idx, pj_war in do_proj(unitsB1, wbB, 3, storeB1, "wb"):
            if idx >= NKV:
                fo = idx - NKV
                for b_ in (0, 1):
                    if pj_war.get(b_) is not None:
                        wt(t, *pj_war[b_])

                def dst3(tc0, n, fo=fo):
                    return dVT[:, tc0:tc0 + n, fo, :]
                transpose_tile(vfm, KEYS, dst3)
        sfree("vfm", "wbB1", "wbB0")
        sfree("xnT_h")
        fence()

        Q_sb = salloc("Q", [128, NQ * TOK], BF16)
        wbQ = [salloc("wbQ0", [128, D], BF16), salloc("wbQ1", [128, D], BF16)]
        unitsB2 = [(wq4, i) for i in range(NQ)]

        def storeB2(idx, tb):
            o = idx * TOK + tb * 512
            return Q_sb[:, o:o + 512]

        for idx, pj_war in do_proj(unitsB2, wbQ, 2, storeB2, "wb"):
            pass
        sfree("wbQ1", "wbQ0")
        sfree("xnT_m")
        fence()

        att_sb = salloc("att", [128, NQ * TOK], BF16, side="right")
        masks = salloc("masks", [128, 5 * 640], BF16)
        junk = salloc("junk", [128, 640], BF16)
        pexp = [salloc("pexp0", [128, 640], BF16), salloc("pexp1", [128, 640], BF16)]
        pms = [salloc("pm0", [128, 640], BF16), salloc("pm1", [128, 640], BF16)]
        pts = [salloc("pt0", [128, 640], BF16), salloc("pt1", [128, 640], BF16)]
        bias = [salloc("bias0", [128, 1], F32), salloc("bias1", [128, 1], F32)]
        rsum = [salloc("rs0", [128, 1], F32), salloc("rs1", [128, 1], F32)]
        rnrm = [salloc("rn0", [128, 1], F32), salloc("rn1", [128, 1], F32)]
        dwt(sy, cst_tag)
        cst_tag2 = ddma("cst", sy, masks.rearrange("p (v m) -> p v m", v=5),
                        maskd.rearrange("v p m -> p v m"))
        dwt(v, cst_tag2)
        mask3 = masks.rearrange("p (m w) -> p m w", m=5)

        last_ptevac = [None, None]
        last_attevac = None
        u = 0
        for qh in range(NQ):
            kvh = qh // 4
            for qc in range(8):
                par = u % 2
                so = 1024 * par
                if last_ptevac[par] is not None:
                    wt(t, *last_ptevac[par])
                lhs = Q_sb[:, qh * TOK + qc * 128: qh * TOK + qc * 128 + 128]
                kb = kvh * KEYS + qc * 128
                t.matmul(psum[:, so:so + 512], lhs, K_sb[:, kb:kb + 512],
                         start=True, stop=True)
                ins = t.matmul(psum[:, so + 512:so + 640], lhs,
                               K_sb[:, kb + 512:kb + 640], start=True, stop=True)
                inc("t", ins)
                wt(v, "t")
                ins = v.tensor_scalar(junk[:], psum[:, so:so + 640], -SCALE, None,
                                      OP.mult, OP.min, accum_out=bias[par][:])
                inc("v", ins)
                wt(a, "v")
                ins = a.activation(pexp[par][:], psum[:, so:so + 640], AF.Exp,
                                   bias=bias[par][:], scale=SCALE)
                inc("a", ins)
                mvv = min(qc, 4)
                wt(v, "a")
                ins = v.scalar_tensor_tensor(pms[par][:], pexp[par][:], 1.0,
                                             mask3[:, mvv, :], OP.mult, OP.mult,
                                             accum_out=rsum[par][:])
                inc("v", ins)
                wt(v, "v")
                inc("v", v.reciprocal(rnrm[par][:], rsum[par][:]))
                wt(v, "v")
                ins = v.tensor_scalar(pexp[par][:], pms[par][:], rnrm[par][:],
                                      None, OP.mult, OP.bypass)
                inc("v", ins)
                wt(t, "v")
                for j in range(5):
                    ins = t.matmul(psum[:, so + 128 * j: so + 128 * j + 128],
                                   pexp[par][:, j * 128:j * 128 + 128], ident[:],
                                   start=True, stop=True)
                inc("t", ins)
                wt(v, "t")
                ins = v.tensor_copy(pts[par][:], psum[:, so:so + 640])
                last_ptevac[par] = ("v", inc("v", ins))
                wt(t, *last_ptevac[par])
                if u % 4 == 0 and last_attevac is not None:
                    wt(t, *last_attevac)
                for j in range(5):
                    tc = qc + j
                    ins = t.matmul(bk(4, (u % 4) * 128, 128),
                                   VT_sb[:, (tc * NKV + kvh) * 128:
                                         (tc * NKV + kvh) * 128 + 128],
                                   pts[par][:, j * 128:j * 128 + 128],
                                   start=(j == 0), stop=(j == 4))
                inc("t", ins)
                if u % 4 == 3:
                    wt(a, "t")
                    dst = att_sb[:, qh * TOK + (qc - 3) * 128:
                                 qh * TOK + (qc - 3) * 128 + 512]
                    ins = a.activation(dst, bk(4), AF.Copy)
                    last_attevac = ("a", inc("a", ins))
                u += 1
        sfree("rn1", "rn0", "rs1", "rs0", "bias1", "bias0", "pt1", "pt0",
              "pm1", "pm0", "pexp1", "pexp0", "junk", "masks")
        sfree("Q", "VT", "K")
        fence()

        wo_out = salloc("wo_out", [128, 8 * D], BF16)
        sqd = salloc("sqd", [128, 512], BF16)
        wors = [salloc(f"wor{i}", [128, 16 * 512], BF16) for i in range(3)]
        wor_war = [0, 0, 0]
        wd_half = {}

        def load_wo(hi):
            fg, hf = divmod(hi, 2)
            ring = hi % 3
            wt(g, "t", wor_war[ring])
            wd_half[hi] = ddma(
                f"wo{ring}", g, wors[ring].rearrange("p (h b) -> p h b", h=16),
                wo4[fg][16 * hf:16 * hf + 16].rearrange("h p b -> p h b"))

        load_wo(0)
        load_wo(1)
        wo_bank_war = {}
        for fg in range(8):
            if 2 * fg + 2 < 16:
                load_wo(2 * fg + 2)
            dwt(t, wd_half[2 * fg])
            dwt(t, wd_half[2 * fg + 1])
            for tc in range(8):
                b_ = tc % 2
                if wo_bank_war.get(b_) is not None:
                    wt(t, *wo_bank_war[b_])
                for ac in range(32):
                    ring = (2 * fg + ac // 16) % 3
                    ins = t.matmul(
                        bk(b_),
                        att_sb[:, ac * TOK + tc * 128: ac * TOK + tc * 128 + 128],
                        wors[ring][:, (ac % 16) * 512:(ac % 16) * 512 + 512],
                        start=(ac == 0), stop=(ac == 31))
                inc("t", ins)
                wt(a, "t")
                dst = wo_out[:, tc * D + fg * 512: tc * D + fg * 512 + 512]
                inc("a", a.activation(dst, bk(b_), AF.Copy,
                                      accum_out=wos1[:, tc * 8 + fg: tc * 8 + fg + 1]))
                ins = a.activation(sqd[:], bk(b_), AF.Square,
                                   accum_out=wos2[:, tc * 8 + fg: tc * 8 + fg + 1])
                wo_bank_war[b_] = ("a", inc("a", ins))
            wor_war[(2 * fg) % 3] = N["t"]
            wor_war[(2 * fg + 1) % 3] = N["t"]
            if 2 * fg + 3 < 16:
                load_wo(2 * fg + 3)
        inc("a", a.activation(sqd[:, 0:1], eps_sb[:], AF.Copy))
        sfree("wor2", "wor1", "wor0", "sqd")
        sfree("att")
        fence()

        x1nT = salloc("x1nT", [128, HC * TOK], BF16, side="right")
        xbD = [salloc("xbD0", [128, D], F32), salloc("xbD1", [128, D], F32)]
        x1n0 = salloc("x1nD", [128, D], BF16)
        wt(v, "a")
        for tc in range(8):
            inc("v", v.tensor_reduce(sum1[:], wos1[:, tc * 8:tc * 8 + 8],
                                     axis=mybir.AxisListType.X, op=OP.add))
            inc("v", v.tensor_reduce(sum2[:], wos2[:, tc * 8:tc * 8 + 8],
                                     axis=mybir.AxisListType.X, op=OP.add))
            wt(v, "v")
            inc("v", v.tensor_scalar(mu1[:], sum1[:], 1.0 / D, None,
                                     OP.mult, OP.bypass))
            wt(v, "v")
            inc("v", v.scalar_tensor_tensor(std1[:], mu1[:], 1.0, mu1[:],
                                            OP.mult, OP.mult))
            wt(v, "v")
            inc("v", v.tensor_scalar(mv2[:, 1:2], sum2[:], 1.0 / D, std1[:],
                                     OP.mult, OP.subtract))
            wt(a, "v")
            inc("a", a.activation(std1[:], mv2[:, 1:2], AF.Sqrt,
                                  bias=eps_sb[:], scale=1.0))
            wt(v, "a")
            inc("v", v.reciprocal(r2a[:, tc:tc + 1], std1[:]))
            wt(v, "v")
            inc("v", v.scalar_tensor_tensor(nmr2a[:, tc:tc + 1], mu1[:], -1.0,
                                            r2a[:, tc:tc + 1], OP.mult, OP.mult))
            wt(v, "v")

        ldD = {}
        xbD_war = [0, 0]
        stD = {}
        x1st_at = []

        def loadD(tc):
            wt(sy, "v", xbD_war[tc % 2])
            dwt(sy, stD.get(tc % 2))
            ldD[tc] = ddma(f"xd{tc % 2}", sy, xbD[tc % 2][:],
                           xm[tc * 128:tc * 128 + 128, :])

        loadD(0)
        loadD(1)
        dx1 = x1nT.rearrange("p (h w) -> p h w", h=HC)
        for tc in range(8):
            x_ = xbD[tc % 2]
            dwt(v, ldD[tc])
            inc("v", v.tensor_scalar(x_[:], x_[:], nmr2a[:, tc:tc + 1], None,
                                     OP.add, OP.bypass))
            ins = v.scalar_tensor_tensor(x_[:], wo_out[:, tc * D:(tc + 1) * D],
                                         r2a[:, tc:tc + 1], x_[:], OP.mult, OP.add)
            inc("v", ins)
            wt(g, "v")
            stD[tc % 2] = ddma(f"xs{tc % 2}", g,
                               x1d[tc * 128:tc * 128 + 128, :], x_[:])
            x1st_at.append(stD[tc % 2])
            ln_prep(x_)
            inc("v", v.tensor_scalar(x1n0[:], x_[:], r1[:], nmr1[:],
                                     OP.mult, OP.add))
            wt(t, "v")

            def dst3(h0, n, tc=tc):
                return dx1[:, h0:h0 + n, tc * 128:tc * 128 + 128]
            transpose_tile(x1n0, D, dst3)
            xbD_war[tc % 2] = N["v"]
            wt(v, "t")
            if tc + 2 < 8:
                loadD(tc + 2)
        sfree("x1nD", "xbD1", "xbD0", "wo_out")
        fence()

        h_sb = salloc("h_sb", [128, IC * 512], BF16)
        qb = {"G": [salloc("qgb0", [128, 1024], I32), salloc("qgb1", [128, 1024], I32)],
              "U": [salloc("qub0", [128, 1024], I32), salloc("qub1", [128, 1024], I32)]}
        szb = [salloc("szb0", [128, 128], F32), salloc("szb1", [128, 128], F32)]
        dqs = [salloc("dqE0", [128, 1024], BF16), salloc("dqE1", [128, 1024], BF16)]
        wgts = [salloc("wgt0", [128, 1024], BF16), salloc("wgt1", [128, 1024], BF16)]
        sgb = [salloc("sg0", [128, 512], BF16), salloc("sg1", [128, 512], BF16)]
        hsp = [salloc(f"hsp{i}", [128, 512], BF16) for i in range(4)]

        deq_eng = []
        for k in range(16):
            deq_eng.append("v" if k < 9 else ("a" if k < 13 else "p"))
        deq_idx = 0

        gran = [(ic, m, gr) for ic in range(IC) for m in ("G", "U")
                for gr in range(4)]
        ldE = {}
        qE_war = {}
        szE_war = [0, 0]

        def loadE_sz(ic):
            wt(sy, "t", szE_war[ic % 2])
            ldE[("sz", ic)] = ddma(
                f"sz{ic % 2}", sy, szb[ic % 2].rearrange("p (k m) -> p k m", k=4),
                szgu[ic].rearrange("k p m -> p k m"))

        def loadE(gi):
            ic, m, gr = gran[gi]
            wt(sy, "t", qE_war.get((m, gr % 2), 0))
            src = qg5 if m == "G" else qu5
            ldE[gi] = ddma(f"q{m}{gr % 2}", sy, qb[m][gr % 2][:],
                           src[ic][:, gr * 1024:gr * 1024 + 1024])

        PRE = 2
        loadE_sz(0)
        loadE_sz(1)
        for j in range(PRE):
            loadE(j)

        negz_at = {}
        nzs_at = {}
        sg_war = [0, 0]
        dq_war = [0, 0]
        mm_bank_war = {}
        spill_at = {}
        wready_at = {}
        deq_state = {"idx": deq_idx}

        def prepE(j):
            ic, m, gr = gran[j]
            scol = szb[ic % 2][:, (0 if m == "G" else 64):
                               (0 if m == "G" else 64) + 32]
            zcol = szb[ic % 2][:, (32 if m == "G" else 96):
                               (32 if m == "G" else 96) + 32]
            if gr == 0:
                dwt(v, ldE[("sz", ic)])
                inc("v", v.tensor_scalar(zcol, zcol, -1.0, None,
                                         OP.mult, OP.bypass))
                negz_at[(ic, m)] = N["v"]
                wt(v, "v")
                wt(v, "a")
                ins = v.scalar_tensor_tensor(nzs[:, (0 if m == "G" else 32):
                                             (0 if m == "G" else 32) + 32],
                                             zcol, 1.0, scol, OP.mult, OP.mult)
                nzs_at[(ic, m)] = inc("v", ins)
            qbuf = qb[m][gr % 2]
            dq = dqs[gr % 2]
            wgt = wgts[gr % 2]
            ldtag = ldE[j]
            done = {}
            for k in range(8):
                hh = gr * 8 + k
                eng = deq_eng[deq_state["idx"] % 16]
                deq_state["idx"] += 1
                sl = slice(k * 128, k * 128 + 128)
                zc = zcol[:, hh:hh + 1]
                sc = scol[:, hh:hh + 1]
                if eng == "v":
                    dwt(v, ldtag)
                    wt(v, "v", negz_at[(ic, m)])
                    wt(v, "t", dq_war[gr % 2])
                    ins = v.tensor_scalar(dq[:, sl], qbuf[:, sl], zc, sc,
                                          OP.add, OP.mult)
                    done["v"] = inc("v", ins)
                elif eng == "a":
                    dwt(a, ldtag)
                    wt(a, "v", nzs_at[(ic, m)])
                    wt(a, "t", dq_war[gr % 2])
                    nb = nzs[:, (0 if m == "G" else 32) + hh:
                             (0 if m == "G" else 32) + hh + 1]
                    ins = a.activation(dq[:, sl], qbuf[:, sl], AF.Identity,
                                       scale=sc, bias=nb)
                    done["a"] = inc("a", ins)
                else:
                    dwt(g, ldtag)
                    wt(g, "v", negz_at[(ic, m)])
                    wt(g, "t", dq_war[gr % 2])
                    ins = g.tensor_scalar(dq[:, sl], qbuf[:, sl], zc, sc,
                                          OP.add, OP.mult)
                    done["p"] = inc("p", ins)
            wready = []
            for half in range(2):
                b_ = 4 + 2 * (j % 2) + half
                if tp_war.get(b_) is not None:
                    wt(t, *tp_war[b_])
                for s_, cnt in done.items():
                    wt(t, s_, cnt)
                for k in range(4):
                    kk = half * 4 + k
                    ins = t.matmul(bk(b_, 128 * k, 128),
                                   dq[:, kk * 128:kk * 128 + 128],
                                   ident[:], start=True, stop=True)
                inc("t", ins)
                eng2, s2 = (v, "v") if half == 0 else (a, "a")
                wt(eng2, "t")
                dst = wgt[:, half * 512:half * 512 + 512]
                if eng2 is v:
                    ins = v.tensor_copy(dst, bk(b_))
                else:
                    ins = a.activation(dst, bk(b_), AF.Copy)
                cnt2 = inc(s2, ins)
                tp_war[b_] = (s2, cnt2)
                wready.append((s2, cnt2))
            dq_war[gr % 2] = N["t"]
            wready_at[j] = wready

        def mmE(j):
            ic, m, gr = gran[j]
            wgt = wgts[gr % 2]
            bb = 0 if m == "G" else 2
            for s2, cnt2 in wready_at.pop(j):
                wt(t, s2, cnt2)
            for tb in range(2):
                b_ = bb + tb
                if gr == 0 and mm_bank_war.get(b_) is not None:
                    wt(t, *mm_bank_war[b_])
                for k in range(8):
                    hh = gr * 8 + k
                    ins = t.matmul(
                        bk(b_), wgt[:, k * 128:k * 128 + 128],
                        x1nT[:, hh * TOK + tb * 512: hh * TOK + tb * 512 + 512],
                        start=(hh == 0), stop=(hh == HC - 1))
                inc("t", ins)
            qE_war[(m, gr % 2)] = N["t"]
            if j + PRE < len(gran):
                loadE(j + PRE)

        prepE(0)
        gi = 0
        for ic in range(IC):
            for m in ("G", "U"):
                for gr in range(4):
                    j = gi
                    gi += 1
                    if j + 1 < len(gran):
                        prepE(j + 1)
                    mmE(j)
                if m == "U":
                    for tb in range(2):
                        wt(a, "t")
                        if ic >= 1:
                            wt(a, "v", sg_war[tb])
                        inc("a", a.activation(sgb[tb][:], bk(0 + tb), AF.Silu))
                        wt(v, "a")
                        if tb == 0:
                            hdst = h_sb[:, ic * 512:(ic + 1) * 512]
                        else:
                            dwt(v, spill_at.get(ic % 4))
                            hdst = hsp[ic % 4][:]
                        ins = v.tensor_tensor(hdst, sgb[tb][:], bk(2 + tb), OP.mult)
                        cnt = inc("v", ins)
                        sg_war[tb] = cnt
                        if tb == 1:
                            wt(g, "v", cnt)
                            spill_at[ic % 4] = ddma(f"hs{ic % 4}", g,
                                                    hbd[:, ic, :], hsp[ic % 4][:])
                    for b_ in range(4):
                        mm_bank_war[b_] = ("v", N["v"])
            szE_war[ic % 2] = N["t"]
            if ic + 2 < IC:
                loadE_sz(ic + 2)
        spill_done = [spill_at[k] for k in spill_at]
        sfree("hsp3", "hsp2", "hsp1", "hsp0", "sg1", "sg0", "wgt1", "wgt0",
              "dqE1", "dqE0", "szb1", "szb0", "qub1", "qub0", "qgb1", "qgb0")
        sfree("x1nT")
        fence()

        ffn_sb = salloc("ffn", [128, 32 * TOK], BF16, side="right")
        qdb = [[salloc("qdb00", [128, 1024], I32), salloc("qdb01", [128, 1024], I32)],
               [salloc("qdb10", [128, 1024], I32), salloc("qdb11", [128, 1024], I32)]]
        hbb = [salloc("hbb0", [128, 4096], BF16), salloc("hbb1", [128, 4096], BF16)]
        szdb = [salloc("szdb0", [128, 344], F32), salloc("szdb1", [128, 344], F32)]
        dqd = [salloc("dqd0", [128, 256], BF16), salloc("dqd1", [128, 256], BF16)]
        wdt = [salloc("wdt0", [128, 256], BF16), salloc("wdt1", [128, 256], BF16)]

        nblk = 11
        ldF = {}
        qF_war = {}
        hF_war = {}
        szF_war = [0, 0]

        def loadF_sz(grp):
            wt(sy, "t", szF_war[grp % 2])
            ldF[("sz", grp)] = ddma(
                f"szd{grp % 2}", sy,
                szdb[grp % 2].rearrange("p (a b m) -> p a b m", a=2, b=2),
                szd[grp].rearrange("a b p m -> p a b m"))

        def loadF(grp, bi):
            ic0 = bi * 8
            nic = min(8, IC - ic0)
            par = bi % 2
            tags = []
            for fsub in range(2):
                fc = grp * 2 + fsub
                wt(sy, "t", qF_war.get((fsub, par), 0))
                tags.append(ddma(
                    f"qd{fsub}{par}", sy, qdb[fsub][par][:, 0:nic * 128],
                    qd5[fc][:, ic0 * 128: ic0 * 128 + nic * 128]))
            wt(sy, "t", hF_war.get(par, 0))
            for sp in spill_done:
                dwt(sy, sp)
            tags.append(ddma(
                f"hb{par}", sy, hbb[par][:, 0:nic * 512],
                hbd[:, ic0:ic0 + nic, :].rearrange("p i m -> p (i m)")))
            ldF[(grp, bi)] = tags

        loadF_sz(0)
        loadF(0, 0)
        dqF_war = [0, 0]
        tpd_war = {}
        dn_bank_war = {}
        negzF = {}
        wreadyF = {}

        def prepF(grp, ic):
            bi = ic // 8
            ii = ic % 8
            par = bi % 2
            ldtag = ldF[(grp, bi)]
            dq = dqd[ic % 2]
            done = {}
            for fsub in range(2):
                eng = "v" if (ic + fsub) % 3 else "p"
                zc = szdb[grp % 2][:, fsub * 172 + 86 + ic:
                                   fsub * 172 + 86 + ic + 1]
                sc = szdb[grp % 2][:, fsub * 172 + ic: fsub * 172 + ic + 1]
                qsl = qdb[fsub][par][:, ii * 128: ii * 128 + 128]
                dsl = dq[:, fsub * 128:fsub * 128 + 128]
                if eng == "v":
                    for tg in ldtag:
                        dwt(v, tg)
                    wt(v, "v", negzF[grp])
                    wt(v, "t", dqF_war[ic % 2])
                    ins = v.tensor_scalar(dsl, qsl, zc, sc, OP.add, OP.mult)
                    done["v"] = inc("v", ins)
                else:
                    for tg in ldtag:
                        dwt(g, tg)
                    wt(g, "v", negzF[grp])
                    wt(g, "t", dqF_war[ic % 2])
                    ins = g.tensor_scalar(dsl, qsl, zc, sc, OP.add, OP.mult)
                    done["p"] = inc("p", ins)
            b_ = 4 + (ic % 2)
            if tpd_war.get(b_) is not None:
                wt(t, *tpd_war[b_])
            for s_, cnt in done.items():
                wt(t, s_, cnt)
            for fsub in range(2):
                ins = t.matmul(bk(b_, fsub * 128, 128),
                               dq[:, fsub * 128:fsub * 128 + 128],
                               ident[:], start=True, stop=True)
            dqF_war[ic % 2] = inc("t", ins)
            eng2, s2 = (v, "v") if ic % 2 == 0 else (a, "a")
            wt(eng2, "t")
            wd_ = wdt[ic % 2]
            if eng2 is v:
                ins = v.tensor_copy(wd_[:], bk(b_, 0, 256))
            else:
                ins = a.activation(wd_[:], bk(b_, 0, 256), AF.Copy)
            tpd_war[b_] = (s2, inc(s2, ins))
            wreadyF[ic] = tpd_war[b_]

        def mmF(grp, ic):
            bi = ic // 8
            ii = ic % 8
            par = bi % 2
            wd_ = wdt[ic % 2]
            wt(t, *wreadyF.pop(ic))
            for fsub in range(2):
                for th in range(2):
                    b2 = fsub * 2 + th
                    if ic == 0 and dn_bank_war.get(b2) is not None:
                        wt(t, *dn_bank_war[b2])
                    rhs = h_sb[:, ic * 512:(ic + 1) * 512] if th == 0 \
                        else hbb[par][:, ii * 512: ii * 512 + 512]
                    ins = t.matmul(bk(b2),
                                   wd_[:, fsub * 128:fsub * 128 + 128],
                                   rhs, start=(ic == 0), stop=(ic == IC - 1))
            inc("t", ins)
            for fsub in range(2):
                qF_war[(fsub, par)] = N["t"]
            hF_war[par] = N["t"]

        for grp in range(16):
            if grp + 1 < 16:
                loadF_sz(grp + 1)
            dwt(v, ldF[("sz", grp)])
            inc("v", v.tensor_scalar(szdb[grp % 2][:, 86:172],
                                     szdb[grp % 2][:, 86:172], -1.0, None,
                                     OP.mult, OP.bypass))
            ins = v.tensor_scalar(szdb[grp % 2][:, 258:344],
                                  szdb[grp % 2][:, 258:344], -1.0, None,
                                  OP.mult, OP.bypass)
            negzF[grp] = inc("v", ins)
            prepF(grp, 0)
            for ic in range(IC):
                bi = ic // 8
                if ic % 8 == 0:
                    if bi + 1 < nblk:
                        loadF(grp, bi + 1)
                    elif grp + 1 < 16:
                        pass
                if ic + 1 < IC:
                    prepF(grp, ic + 1)
                mmF(grp, ic)
                if ic == IC - 1 and grp + 1 < 16:
                    loadF(grp + 1, 0)
            szF_war[grp % 2] = N["t"]
            for fsub in range(2):
                for th in range(2):
                    b2 = fsub * 2 + th
                    fck = grp * 2 + fsub
                    wt(a, "t")
                    dst = ffn_sb[:, fck * TOK + th * 512: fck * TOK + th * 512 + 512]
                    ins = a.activation(dst, bk(b2), AF.Copy)
                    dn_bank_war[b2] = ("a", inc("a", ins))
        sfree("wdt1", "wdt0", "dqd1", "dqd0", "szdb1", "szdb0",
              "hbb1", "hbb0", "qdb11", "qdb10", "qdb01", "qdb00", "h_sb")
        fence()

        fT0 = salloc("fT0", [128, D], BF16)
        x1t = [salloc("x1t0", [128, D], F32), salloc("x1t1", [128, D], F32)]
        ldG = {}
        x1t_war = [0, 0]
        stG = {}

        def loadG(tc):
            wt(sy, "v", x1t_war[tc % 2])
            dwt(sy, stG.get(tc % 2))
            dwt(sy, x1st_at[tc])
            ldG[tc] = ddma(f"xg{tc % 2}", sy, x1t[tc % 2][:],
                           x1d[tc * 128:tc * 128 + 128, :])

        loadG(0)
        loadG(1)
        ffn3 = ffn_sb.rearrange("p (f w) -> p f w", f=32)
        fdst = fT0.rearrange("p (f w) -> p f w", f=32)
        fT_war = 0
        for tc in range(8):
            for grp in range(8):
                b_ = 4 + grp % 2
                if tp_war.get(b_) is not None:
                    wt(t, *tp_war[b_])
                wt(t, "v", fT_war)
                for k in range(4):
                    f = grp * 4 + k
                    ins = t.matmul(bk(b_, 128 * k, 128),
                                   ffn3[:, f, tc * 128:tc * 128 + 128],
                                   ident[:], start=True, stop=True)
                inc("t", ins)
                eng, s_ = (v, "v") if grp % 2 == 0 else (a, "a")
                wt(eng, "t")
                pv = bk(b_).rearrange("p (k m) -> p k m", k=4)
                if eng is v:
                    ins = v.tensor_copy(fdst[:, grp * 4:grp * 4 + 4, :], pv)
                else:
                    ins = a.activation(fdst[:, grp * 4:grp * 4 + 4, :], pv, AF.Copy)
                tp_war[b_] = (s_, inc(s_, ins))
            wt(v, "a")
            wt(v, "v")
            ln_prep(fT0)
            dwt(v, ldG[tc])
            x_ = x1t[tc % 2]
            inc("v", v.tensor_scalar(x_[:], x_[:], nmr1[:], None, OP.add, OP.bypass))
            ins = v.scalar_tensor_tensor(x_[:], fT0[:], r1[:], x_[:],
                                         OP.mult, OP.add)
            fT_war = inc("v", ins)
            x1t_war[tc % 2] = N["v"]
            wt(g, "v")
            stG[tc % 2] = ddma(f"os{tc % 2}", g,
                               outd[tc * 128:tc * 128 + 128, :], x_[:])
            if tc + 2 < 8:
                loadG(tc + 2)
        sfree("x1t1", "x1t0", "fT0")
        for nm in sorted(dslots):
            sy.wait_ge(dslots[nm][0], dslots[nm][1])

    @blk.sync
    def _(_sync):
        emit()

    st.close()
    return nc


def _prep_inputs(inp):
    f32 = np.float32
    x = np.asarray(inp["L0_attn_x"], f32)
    wq = np.asarray(inp["L0_attn_wq"], f32)
    wk = np.asarray(inp["L0_attn_wk"], f32)
    wv = np.asarray(inp["L0_attn_wv"], f32)
    wo = np.asarray(inp["L0_attn_wo"], f32)
    qg = np.asarray(inp["L0_ffn_qw_gate"], np.int32)
    qu = np.asarray(inp["L0_ffn_qw_up"], np.int32)
    qd = np.asarray(inp["L0_ffn_qw_down"], np.int32)
    sg = np.asarray(inp["L0_ffn_s_gate"], f32)
    zg = np.asarray(inp["L0_ffn_z_gate"], f32)
    su = np.asarray(inp["L0_ffn_s_up"], f32)
    zu = np.asarray(inp["L0_ffn_z_up"], f32)
    sd = np.asarray(inp["L0_ffn_s_down"], f32)
    zd = np.asarray(inp["L0_ffn_z_down"], f32)

    def tile4(wT, nf):
        return np.ascontiguousarray(
            wT.reshape(HC, 128, nf, 128).transpose(2, 1, 0, 3))

    wq4 = tile4(wq.T, NQ)
    wk4 = tile4(wk.T, NKV)
    wv4 = tile4(wv.T, NKV)
    wo4 = np.ascontiguousarray(wo.T.reshape(32, 128, 8, 512).transpose(2, 0, 1, 3))
    qg5 = np.ascontiguousarray(qg.reshape(IC, 128, D))
    qu5 = np.ascontiguousarray(qu.reshape(IC, 128, D))
    qd5 = np.ascontiguousarray(qd.reshape(32, 128, II))
    szgu = np.ascontiguousarray(
        np.stack([sg.reshape(IC, 128, 32), zg.reshape(IC, 128, 32),
                  su.reshape(IC, 128, 32), zu.reshape(IC, 128, 32)], axis=1))
    sd5 = sd.reshape(16, 2, 128, IC)
    zd5 = zd.reshape(16, 2, 128, IC)
    szd_ = np.ascontiguousarray(np.stack([sd5, zd5], axis=2))

    ident = np.eye(128, dtype=ml_dtypes.bfloat16)
    jj = np.arange(640)[None, :]
    pp = np.arange(128)[:, None]
    base = (jj > pp) & (jj <= pp + 512)
    m_even = np.stack([(base & (jj >= 512 - vv * 128)) if vv < 4 else base
                       for vv in range(5)]).astype(ml_dtypes.bfloat16)
    m_odd = np.stack([base] * 5).astype(ml_dtypes.bfloat16)

    in_maps = []
    for core in range(8):
        b, half = divmod(core, 2)
        q0 = half * TOK
        xmain = np.ascontiguousarray(x[b, q0:q0 + TOK])
        xhalo = (np.ascontiguousarray(x[b, q0 - HALO:q0]) if half == 1
                 else np.zeros((HALO, D), f32))
        in_maps.append({
            "xm": xmain, "xh": xhalo,
            "wq4": wq4, "wk4": wk4, "wv4": wv4, "wo4": wo4,
            "qg5": qg5, "qu5": qu5, "qd5": qd5,
            "szgu": szgu, "szd": szd_,
            "maskd": m_odd if half == 1 else m_even,
            "identd": ident,
        })
    return in_maps


def kernel(**inputs):
    in_maps = _prep_inputs(inputs)
    nc = build_core_program()
    res = run_bass_kernel_spmd(nc, in_maps, core_ids=list(range(8)))
    out = np.empty((4, 2048, D), np.float32)
    for core in range(8):
        b, half = divmod(core, 2)
        out[b, half * TOK:(half + 1) * TOK] = res.results[core]["outd"]
    return out


# version 06
# speedup vs baseline: 1.3422x; 1.3402x over previous
import numpy as np
import ml_dtypes
import contextlib
import concourse.bass as bass
import concourse.mybir as mybir
from concourse.alu_op_type import AluOpType
from concourse.bass_utils import run_bass_kernel_spmd

F32 = mybir.dt.float32
BF16 = mybir.dt.bfloat16
I32 = mybir.dt.int32
AF = mybir.ActivationFunctionType
OP = AluOpType

TOK = 1024
HALO = 512
D = 4096
HC = 32
NQ, NKV, HD = 32, 8, 128
II = 11008
IC = 86
SCALE = float(1.0 / np.sqrt(128.0))
EPS = 1e-5
KEYS = TOK + HALO
KC = 12


def build_core_program():
    nc = bass.Bass("TRN2", target_bir_lowering=False)

    xm = nc.dram_tensor("xm", [TOK, D], F32, kind="ExternalInput").ap()
    xh = nc.dram_tensor("xh", [HALO, D], F32, kind="ExternalInput").ap()
    wq4 = nc.dram_tensor("wq4", [NQ, 128, HC, 128], F32, kind="ExternalInput").ap()
    wk4 = nc.dram_tensor("wk4", [NKV, 128, HC, 128], F32, kind="ExternalInput").ap()
    wv4 = nc.dram_tensor("wv4", [NKV, 128, HC, 128], F32, kind="ExternalInput").ap()
    wo4 = nc.dram_tensor("wo4", [8, 32, 128, 512], F32, kind="ExternalInput").ap()
    qg5 = nc.dram_tensor("qg5", [IC, 128, D], I32, kind="ExternalInput").ap()
    qu5 = nc.dram_tensor("qu5", [IC, 128, D], I32, kind="ExternalInput").ap()
    qd5 = nc.dram_tensor("qd5", [32, 128, II], I32, kind="ExternalInput").ap()
    szgu = nc.dram_tensor("szgu", [IC, 4, 128, 32], F32, kind="ExternalInput").ap()
    szd = nc.dram_tensor("szd", [16, 2, 2, 128, IC], F32, kind="ExternalInput").ap()
    maskd = nc.dram_tensor("maskd", [5, 128, 640], BF16, kind="ExternalInput").ap()
    identd = nc.dram_tensor("identd", [128, 128], BF16, kind="ExternalInput").ap()
    outd = nc.dram_tensor("outd", [TOK, D], F32, kind="ExternalOutput").ap()
    x1d = nc.dram_tensor("x1d", [TOK, D], F32).ap()
    hbd = nc.dram_tensor("hbd", [128, IC, 512], BF16).ap()

    st = contextlib.ExitStack()

    SEMS = ["t", "v", "a", "p"]
    H = {s: st.enter_context(nc.semaphore(s)) for s in SEMS}
    N = {s: 0 for s in SEMS}

    def inc(s, instr, k=1):
        instr.then_inc(H[s], k)
        N[s] += k
        return N[s]

    def wt(eng, s, target=None):
        if target is None:
            target = N[s]
        if target > 0:
            eng.wait_ge(H[s], target)

    dslots = {}

    def ddma(name, qeng, dst, src_):
        if name not in dslots:
            dslots[name] = [st.enter_context(nc.semaphore("D" + name)), 0]
        ent = dslots[name]
        qeng.dma_start(dst, src_).then_inc(ent[0], 16)
        ent[1] += 16
        return (name, ent[1])

    def dwt(eng, tag):
        if tag is None:
            return
        nm, target = tag
        eng.wait_ge(dslots[nm][0], target)

    guards = {}

    def salloc(name, shape, dt, side="left"):
        gd = nc.sbuf_tensor(name, shape, dt, side=side)
        h = gd.__enter__()
        guards[name] = gd
        return h

    def sfree(*names):
        for name in names:
            guards.pop(name).__exit__(None, None, None)

    psum = st.enter_context(nc.psum_tensor("ps", [128, 4096], F32))

    def bk(k, off=0, n=512):
        return psum[:, 512 * k + off: 512 * k + off + n]

    ident = salloc("ident", [128, 128], BF16)
    eps_sb = salloc("eps", [128, 1], F32)
    stat6 = salloc("stat6", [128, 48], F32)
    mv2 = salloc("mv2", [128, 2], F32)
    std1 = salloc("std1", [128, 1], F32)
    r1 = salloc("r1", [128, 1], F32)
    nmr1 = salloc("nmr1", [128, 1], F32)
    wos1 = salloc("wos1", [128, 64], F32)
    wos2 = salloc("wos2", [128, 64], F32)
    r2a = salloc("r2a", [128, 8], F32)
    nmr2a = salloc("nmr2a", [128, 8], F32)
    sum1 = salloc("sum1", [128, 1], F32)
    sum2 = salloc("sum2", [128, 1], F32)
    mu1 = salloc("mu1", [128, 1], F32)
    nzs = salloc("nzs", [128, 64], F32)

    v, a, t, g, sy = nc.vector, nc.scalar, nc.tensor, nc.gpsimd, nc.sync

    blk = st.enter_context(nc.Block())

    def fence():
        for eng in (t, v, a, g, sy):
            for s in ("v", "a", "t", "p"):
                wt(eng, s)

    def emit():
        cst_tag = ddma("cst", sy, ident[:], identd)
        dwt(v, cst_tag)
        v.memset(eps_sb[:], EPS)
        inc("v", v.tensor_scalar(eps_sb[:], eps_sb[:], 1.0, None, OP.mult, OP.bypass))

        def ln_prep(src):
            for c in range(8):
                ins = v.bn_stats(stat6[:, 6 * c:6 * c + 6],
                                 src[:, 512 * c:512 * c + 512])
            inc("v", ins)
            wt(v, "v")
            inc("v", v.bn_aggr(mv2[:], stat6[:, 0:48]))
            wt(a, "v")
            inc("a", a.activation(std1[:], mv2[:, 1:2], AF.Sqrt,
                                  bias=eps_sb[:], scale=1.0))
            wt(v, "a")
            inc("v", v.reciprocal(r1[:], std1[:]))
            wt(v, "v")
            inc("v", v.scalar_tensor_tensor(nmr1[:], mv2[:, 0:1], -1.0, r1[:],
                                            OP.mult, OP.mult))
            wt(v, "v")

        tp_war = {}

        def transpose_tile(src, width, dst3):
            nh = width // 128
            for grp in range(nh // 4):
                b_ = 4 + grp % 2
                if tp_war.get(b_) is not None:
                    wt(t, *tp_war[b_])
                for k in range(4):
                    hh = grp * 4 + k
                    ins = t.matmul(bk(b_, 128 * k, 128),
                                   src[:, 128 * hh:128 * hh + 128], ident[:],
                                   start=True, stop=True)
                inc("t", ins)
                eng, s_ = (v, "v") if grp % 2 == 0 else (a, "a")
                wt(eng, "t")
                pv = bk(b_).rearrange("p (k m) -> p k m", k=4)
                if eng is v:
                    ins = v.tensor_copy(dst3(grp * 4, 4), pv)
                else:
                    ins = a.activation(dst3(grp * 4, 4), pv, AF.Copy)
                tp_war[b_] = (s_, inc(s_, ins))

        xnT_m = salloc("xnT_m", [128, HC * TOK], BF16, side="right")
        xnT_h = salloc("xnT_h", [128, HC * HALO], BF16, side="right")
        xb = [salloc("xbA0", [128, D], F32, side="right"),
              salloc("xbA1", [128, D], F32, side="right")]
        xt = [salloc("xtA0", [128, D], BF16, side="right"),
              salloc("xtA1", [128, D], BF16, side="right")]
        tiles = [("m", i) for i in range(8)] + [("h", i) for i in range(4)]
        ld_at = {}
        xb_war = [0, 0]
        xt_war = [0, 0]

        def loadA(i):
            kind, tc = tiles[i]
            src = xm if kind == "m" else xh
            wt(sy, "v", xb_war[i % 2])
            ld_at[i] = ddma(f"xa{i % 2}", sy, xb[i % 2][:],
                            src[tc * 128:tc * 128 + 128, :])

        loadA(0)
        loadA(1)
        dm = xnT_m.rearrange("p (h w) -> p h w", h=HC)
        dh = xnT_h.rearrange("p (h w) -> p h w", h=HC)
        for i, (kind, tc) in enumerate(tiles):
            dwt(v, ld_at[i])
            ln_prep(xb[i % 2])
            wt(v, "t", xt_war[i % 2])
            ins = v.tensor_scalar(xt[i % 2][:], xb[i % 2][:], r1[:], nmr1[:],
                                  OP.mult, OP.add)
            xb_war[i % 2] = inc("v", ins)
            wt(t, "v")
            dst = dm if kind == "m" else dh

            def dst3(h0, n, tc=tc, dst=dst):
                return dst[:, h0:h0 + n, tc * 128:tc * 128 + 128]
            transpose_tile(xt[i % 2], D, dst3)
            xt_war[i % 2] = N["t"]
            if i + 2 < len(tiles):
                loadA(i + 2)
        sfree("xtA1", "xtA0", "xbA1", "xbA0")
        fence()

        K_sb = salloc("K", [128, NKV * KEYS], BF16)
        VT_sb = salloc("VT", [128, KC * NKV * 128], BF16)

        def proj_rhs(hh, tb):
            if tb == 0:
                return xnT_h[:, hh * HALO: hh * HALO + 512]
            o = hh * TOK + (tb - 1) * 512
            return xnT_m[:, o:o + 512]

        def do_proj(units, wbuf, ntb, store, pfx):
            wb_war = [0, 0]
            wd_at = {}
            pj_war = {}

            def loadW(idx):
                src4, fo = units[idx]
                wt(g, "t", wb_war[idx % 2])
                wd_at[idx] = ddma(f"{pfx}{idx % 2}", g, wbuf[idx % 2][:],
                                  src4[fo].rearrange("p h b -> p (h b)"))

            loadW(0)
            if len(units) > 1:
                loadW(1)
            for idx in range(len(units)):
                dwt(t, wd_at[idx])
                for tb in range(ntb):
                    b_ = tb % 2
                    if pj_war.get(b_) is not None:
                        wt(t, *pj_war[b_])
                    for hh in range(HC):
                        ins = t.matmul(bk(b_),
                                       wbuf[idx % 2][:, hh * 128:hh * 128 + 128],
                                       proj_rhs(hh, tb + (3 - ntb)),
                                       start=(hh == 0), stop=(hh == HC - 1))
                    inc("t", ins)
                    eng, s_ = (v, "v") if tb % 2 == 0 else (a, "a")
                    wt(eng, "t")
                    dstap = store(idx, tb)
                    if eng is v:
                        ins = v.tensor_copy(dstap, bk(b_))
                    else:
                        ins = a.activation(dstap, bk(b_), AF.Copy)
                    pj_war[b_] = (s_, inc(s_, ins))
                wb_war[idx % 2] = N["t"]
                if idx + 2 < len(units):
                    loadW(idx + 2)
                yield idx, pj_war

        wbB = [salloc("wbB0", [128, D], BF16), salloc("wbB1", [128, D], BF16)]
        vfm = salloc("vfm", [128, KEYS], BF16)
        unitsB1 = [(wk4, i) for i in range(NKV)] + [(wv4, i) for i in range(NKV)]

        def storeB1(idx, tb):
            if idx < NKV:
                o = idx * KEYS + tb * 512
                return K_sb[:, o:o + 512]
            return vfm[:, tb * 512: tb * 512 + 512]

        dVT = VT_sb.rearrange("p (c e m) -> p c e m", c=KC, e=NKV)
        for idx, pj_war in do_proj(unitsB1, wbB, 3, storeB1, "wb"):
            if idx >= NKV:
                fo = idx - NKV
                for b_ in (0, 1):
                    if pj_war.get(b_) is not None:
                        wt(t, *pj_war[b_])

                def dst3(tc0, n, fo=fo):
                    return dVT[:, tc0:tc0 + n, fo, :]
                transpose_tile(vfm, KEYS, dst3)
        sfree("vfm", "wbB1", "wbB0")
        sfree("xnT_h")
        fence()

        Q_sb = salloc("Q", [128, NQ * TOK], BF16)
        wbQ = [salloc("wbQ0", [128, D], BF16), salloc("wbQ1", [128, D], BF16)]
        unitsB2 = [(wq4, i) for i in range(NQ)]

        def storeB2(idx, tb):
            o = idx * TOK + tb * 512
            return Q_sb[:, o:o + 512]

        for idx, pj_war in do_proj(unitsB2, wbQ, 2, storeB2, "wb"):
            pass
        sfree("wbQ1", "wbQ0")
        sfree("xnT_m")
        fence()

        att_sb = salloc("att", [128, NQ * TOK], BF16, side="right")
        masks = salloc("masks", [128, 5 * 640], BF16)
        junk = salloc("junk", [128, 640], BF16)
        pexp = [salloc("pexp0", [128, 640], BF16), salloc("pexp1", [128, 640], BF16)]
        pms = [salloc("pm0", [128, 640], BF16), salloc("pm1", [128, 640], BF16)]
        pts = [salloc("pt0", [128, 640], BF16), salloc("pt1", [128, 640], BF16)]
        bias = [salloc("bias0", [128, 1], F32), salloc("bias1", [128, 1], F32)]
        rsum = [salloc("rs0", [128, 1], F32), salloc("rs1", [128, 1], F32)]
        rnrm = [salloc("rn0", [128, 1], F32), salloc("rn1", [128, 1], F32)]
        dwt(sy, cst_tag)
        cst_tag2 = ddma("cst", sy, masks.rearrange("p (v m) -> p v m", v=5),
                        maskd.rearrange("v p m -> p v m"))
        dwt(v, cst_tag2)
        mask3 = masks.rearrange("p (m w) -> p m w", m=5)

        last_ptevac = [None, None]
        last_attevac = None
        u = 0
        for qh in range(NQ):
            kvh = qh // 4
            for qc in range(8):
                par = u % 2
                so = 1024 * par
                if last_ptevac[par] is not None:
                    wt(t, *last_ptevac[par])
                lhs = Q_sb[:, qh * TOK + qc * 128: qh * TOK + qc * 128 + 128]
                kb = kvh * KEYS + qc * 128
                t.matmul(psum[:, so:so + 512], lhs, K_sb[:, kb:kb + 512],
                         start=True, stop=True)
                ins = t.matmul(psum[:, so + 512:so + 640], lhs,
                               K_sb[:, kb + 512:kb + 640], start=True, stop=True)
                inc("t", ins)
                wt(v, "t")
                ins = v.tensor_scalar(junk[:], psum[:, so:so + 640], -SCALE, None,
                                      OP.mult, OP.min, accum_out=bias[par][:])
                inc("v", ins)
                wt(a, "v")
                ins = a.activation(pexp[par][:], psum[:, so:so + 640], AF.Exp,
                                   bias=bias[par][:], scale=SCALE)
                inc("a", ins)
                mvv = min(qc, 4)
                wt(v, "a")
                ins = v.scalar_tensor_tensor(pms[par][:], pexp[par][:], 1.0,
                                             mask3[:, mvv, :], OP.mult, OP.mult,
                                             accum_out=rsum[par][:])
                inc("v", ins)
                wt(v, "v")
                inc("v", v.reciprocal(rnrm[par][:], rsum[par][:]))
                wt(v, "v")
                ins = v.tensor_scalar(pexp[par][:], pms[par][:], rnrm[par][:],
                                      None, OP.mult, OP.bypass)
                inc("v", ins)
                wt(t, "v")
                for j in range(5):
                    ins = t.matmul(psum[:, so + 128 * j: so + 128 * j + 128],
                                   pexp[par][:, j * 128:j * 128 + 128], ident[:],
                                   start=True, stop=True)
                inc("t", ins)
                wt(v, "t")
                ins = v.tensor_copy(pts[par][:], psum[:, so:so + 640])
                last_ptevac[par] = ("v", inc("v", ins))
                wt(t, *last_ptevac[par])
                if u % 4 == 0 and last_attevac is not None:
                    wt(t, *last_attevac)
                for j in range(5):
                    tc = qc + j
                    ins = t.matmul(bk(4, (u % 4) * 128, 128),
                                   VT_sb[:, (tc * NKV + kvh) * 128:
                                         (tc * NKV + kvh) * 128 + 128],
                                   pts[par][:, j * 128:j * 128 + 128],
                                   start=(j == 0), stop=(j == 4))
                inc("t", ins)
                if u % 4 == 3:
                    wt(a, "t")
                    dst = att_sb[:, qh * TOK + (qc - 3) * 128:
                                 qh * TOK + (qc - 3) * 128 + 512]
                    ins = a.activation(dst, bk(4), AF.Copy)
                    last_attevac = ("a", inc("a", ins))
                u += 1
        sfree("rn1", "rn0", "rs1", "rs0", "bias1", "bias0", "pt1", "pt0",
              "pm1", "pm0", "pexp1", "pexp0", "junk", "masks")
        sfree("Q", "VT", "K")
        fence()

        wo_out = salloc("wo_out", [128, 8 * D], BF16)
        sqd = salloc("sqd", [128, 512], BF16)
        wors = [salloc(f"wor{i}", [128, 16 * 512], BF16) for i in range(3)]
        wor_war = [0, 0, 0]
        wd_half = {}

        def load_wo(hi):
            fg, hf = divmod(hi, 2)
            ring = hi % 3
            wt(g, "t", wor_war[ring])
            wd_half[hi] = ddma(
                f"wo{ring}", g, wors[ring].rearrange("p (h b) -> p h b", h=16),
                wo4[fg][16 * hf:16 * hf + 16].rearrange("h p b -> p h b"))

        load_wo(0)
        load_wo(1)
        wo_bank_war = {}
        for fg in range(8):
            if 2 * fg + 2 < 16:
                load_wo(2 * fg + 2)
            dwt(t, wd_half[2 * fg])
            dwt(t, wd_half[2 * fg + 1])
            for tc in range(8):
                b_ = tc % 2
                if wo_bank_war.get(b_) is not None:
                    wt(t, *wo_bank_war[b_])
                for ac in range(32):
                    ring = (2 * fg + ac // 16) % 3
                    ins = t.matmul(
                        bk(b_),
                        att_sb[:, ac * TOK + tc * 128: ac * TOK + tc * 128 + 128],
                        wors[ring][:, (ac % 16) * 512:(ac % 16) * 512 + 512],
                        start=(ac == 0), stop=(ac == 31))
                inc("t", ins)
                wt(a, "t")
                dst = wo_out[:, tc * D + fg * 512: tc * D + fg * 512 + 512]
                inc("a", a.activation(dst, bk(b_), AF.Copy,
                                      accum_out=wos1[:, tc * 8 + fg: tc * 8 + fg + 1]))
                ins = a.activation(sqd[:], bk(b_), AF.Square,
                                   accum_out=wos2[:, tc * 8 + fg: tc * 8 + fg + 1])
                wo_bank_war[b_] = ("a", inc("a", ins))
            wor_war[(2 * fg) % 3] = N["t"]
            wor_war[(2 * fg + 1) % 3] = N["t"]
            if 2 * fg + 3 < 16:
                load_wo(2 * fg + 3)
        inc("a", a.activation(sqd[:, 0:1], eps_sb[:], AF.Copy))
        sfree("wor2", "wor1", "wor0", "sqd")
        sfree("att")
        fence()

        x1nT = salloc("x1nT", [128, HC * TOK], BF16, side="right")
        xbD = [salloc("xbD0", [128, D], F32), salloc("xbD1", [128, D], F32)]
        x1n0 = salloc("x1nD", [128, D], BF16)
        wt(v, "a")
        for tc in range(8):
            inc("v", v.tensor_reduce(sum1[:], wos1[:, tc * 8:tc * 8 + 8],
                                     axis=mybir.AxisListType.X, op=OP.add))
            inc("v", v.tensor_reduce(sum2[:], wos2[:, tc * 8:tc * 8 + 8],
                                     axis=mybir.AxisListType.X, op=OP.add))
            wt(v, "v")
            inc("v", v.tensor_scalar(mu1[:], sum1[:], 1.0 / D, None,
                                     OP.mult, OP.bypass))
            wt(v, "v")
            inc("v", v.scalar_tensor_tensor(std1[:], mu1[:], 1.0, mu1[:],
                                            OP.mult, OP.mult))
            wt(v, "v")
            inc("v", v.tensor_scalar(mv2[:, 1:2], sum2[:], 1.0 / D, std1[:],
                                     OP.mult, OP.subtract))
            wt(a, "v")
            inc("a", a.activation(std1[:], mv2[:, 1:2], AF.Sqrt,
                                  bias=eps_sb[:], scale=1.0))
            wt(v, "a")
            inc("v", v.reciprocal(r2a[:, tc:tc + 1], std1[:]))
            wt(v, "v")
            inc("v", v.scalar_tensor_tensor(nmr2a[:, tc:tc + 1], mu1[:], -1.0,
                                            r2a[:, tc:tc + 1], OP.mult, OP.mult))
            wt(v, "v")

        ldD = {}
        xbD_war = [0, 0]
        stD = {}
        x1st_at = []

        def loadD(tc):
            wt(sy, "v", xbD_war[tc % 2])
            dwt(sy, stD.get(tc % 2))
            ldD[tc] = ddma(f"xd{tc % 2}", sy, xbD[tc % 2][:],
                           xm[tc * 128:tc * 128 + 128, :])

        loadD(0)
        loadD(1)
        dx1 = x1nT.rearrange("p (h w) -> p h w", h=HC)
        for tc in range(8):
            x_ = xbD[tc % 2]
            dwt(v, ldD[tc])
            inc("v", v.tensor_scalar(x_[:], x_[:], nmr2a[:, tc:tc + 1], None,
                                     OP.add, OP.bypass))
            ins = v.scalar_tensor_tensor(x_[:], wo_out[:, tc * D:(tc + 1) * D],
                                         r2a[:, tc:tc + 1], x_[:], OP.mult, OP.add)
            inc("v", ins)
            wt(g, "v")
            stD[tc % 2] = ddma(f"xs{tc % 2}", g,
                               x1d[tc * 128:tc * 128 + 128, :], x_[:])
            x1st_at.append(stD[tc % 2])
            ln_prep(x_)
            inc("v", v.tensor_scalar(x1n0[:], x_[:], r1[:], nmr1[:],
                                     OP.mult, OP.add))
            wt(t, "v")

            def dst3(h0, n, tc=tc):
                return dx1[:, h0:h0 + n, tc * 128:tc * 128 + 128]
            transpose_tile(x1n0, D, dst3)
            xbD_war[tc % 2] = N["v"]
            wt(v, "t")
            if tc + 2 < 8:
                loadD(tc + 2)
        sfree("x1nD", "xbD1", "xbD0", "wo_out")
        fence()

        h_sb = salloc("h_sb", [128, IC * 512], BF16)
        qb = {"G": [salloc(f"qgb{i}", [128, 1024], I32) for i in range(4)],
              "U": [salloc(f"qub{i}", [128, 1024], I32) for i in range(4)]}
        szb = [salloc("szb0", [128, 128], F32), salloc("szb1", [128, 128], F32)]
        dqs = [salloc("dqE0", [128, 1024], BF16), salloc("dqE1", [128, 1024], BF16)]
        wgts = [salloc("wgt0", [128, 1024], BF16), salloc("wgt1", [128, 1024], BF16)]
        sgb = [salloc("sg0", [128, 512], BF16), salloc("sg1", [128, 512], BF16)]
        hsp = [salloc(f"hsp{i}", [128, 512], BF16) for i in range(4)]

        deq_eng = []
        for k in range(16):
            deq_eng.append("v" if k < 9 else ("a" if k < 13 else "p"))
        deq_idx = 0

        gran = [(ic, m, gr) for ic in range(IC) for m in ("G", "U")
                for gr in range(4)]
        ldE = {}
        qE_war = {}
        szE_war = [0, 0]

        def loadE_sz(ic):
            wt(sy, "t", szE_war[ic % 2])
            ldE[("sz", ic)] = ddma(
                f"sz{ic % 2}", sy, szb[ic % 2].rearrange("p (k m) -> p k m", k=4),
                szgu[ic].rearrange("k p m -> p k m"))

        def loadE(gi):
            ic, m, gr = gran[gi]
            wt(sy, "t", qE_war.get((m, gr), 0))
            src = qg5 if m == "G" else qu5
            ldE[gi] = ddma(f"q{m}{gr}", sy, qb[m][gr][:],
                           src[ic][:, gr * 1024:gr * 1024 + 1024])

        PRE = 6
        loadE_sz(0)
        loadE_sz(1)
        for j in range(PRE):
            loadE(j)

        negz_at = {}
        nzs_at = {}
        sg_war = [0, 0]
        dq_war = [0, 0]
        mm_bank_war = {}
        spill_at = {}
        wready_at = {}
        deq_state = {"idx": deq_idx}

        def prepE(j):
            ic, m, gr = gran[j]
            scol = szb[ic % 2][:, (0 if m == "G" else 64):
                               (0 if m == "G" else 64) + 32]
            zcol = szb[ic % 2][:, (32 if m == "G" else 96):
                               (32 if m == "G" else 96) + 32]
            if gr == 0:
                dwt(v, ldE[("sz", ic)])
                inc("v", v.tensor_scalar(zcol, zcol, -1.0, None,
                                         OP.mult, OP.bypass))
                negz_at[(ic, m)] = N["v"]
                wt(v, "v")
                wt(v, "a")
                ins = v.scalar_tensor_tensor(nzs[:, (0 if m == "G" else 32):
                                             (0 if m == "G" else 32) + 32],
                                             zcol, 1.0, scol, OP.mult, OP.mult)
                nzs_at[(ic, m)] = inc("v", ins)
            qbuf = qb[m][gr]
            dq = dqs[gr % 2]
            wgt = wgts[gr % 2]
            ldtag = ldE[j]
            done = {}
            for k in range(8):
                hh = gr * 8 + k
                eng = deq_eng[deq_state["idx"] % 16]
                deq_state["idx"] += 1
                sl = slice(k * 128, k * 128 + 128)
                zc = zcol[:, hh:hh + 1]
                sc = scol[:, hh:hh + 1]
                if eng == "v":
                    dwt(v, ldtag)
                    wt(v, "v", negz_at[(ic, m)])
                    wt(v, "t", dq_war[gr % 2])
                    ins = v.tensor_scalar(dq[:, sl], qbuf[:, sl], zc, sc,
                                          OP.add, OP.mult)
                    done["v"] = inc("v", ins)
                elif eng == "a":
                    dwt(a, ldtag)
                    wt(a, "v", nzs_at[(ic, m)])
                    wt(a, "t", dq_war[gr % 2])
                    nb = nzs[:, (0 if m == "G" else 32) + hh:
                             (0 if m == "G" else 32) + hh + 1]
                    ins = a.activation(dq[:, sl], qbuf[:, sl], AF.Identity,
                                       scale=sc, bias=nb)
                    done["a"] = inc("a", ins)
                else:
                    dwt(g, ldtag)
                    wt(g, "v", negz_at[(ic, m)])
                    wt(g, "t", dq_war[gr % 2])
                    ins = g.tensor_scalar(dq[:, sl], qbuf[:, sl], zc, sc,
                                          OP.add, OP.mult)
                    done["p"] = inc("p", ins)
            wready = []
            for half in range(2):
                b_ = 4 + 2 * (j % 2) + half
                if tp_war.get(b_) is not None:
                    wt(t, *tp_war[b_])
                for s_, cnt in done.items():
                    wt(t, s_, cnt)
                for k in range(4):
                    kk = half * 4 + k
                    ins = t.matmul(bk(b_, 128 * k, 128),
                                   dq[:, kk * 128:kk * 128 + 128],
                                   ident[:], start=True, stop=True)
                inc("t", ins)
                eng2, s2 = (v, "v") if half == 0 else (a, "a")
                wt(eng2, "t")
                dst = wgt[:, half * 512:half * 512 + 512]
                if eng2 is v:
                    ins = v.tensor_copy(dst, bk(b_))
                else:
                    ins = a.activation(dst, bk(b_), AF.Copy)
                cnt2 = inc(s2, ins)
                tp_war[b_] = (s2, cnt2)
                wready.append((s2, cnt2))
            dq_war[gr % 2] = N["t"]
            wready_at[j] = wready

        def mmE(j):
            ic, m, gr = gran[j]
            wgt = wgts[gr % 2]
            bb = 0 if m == "G" else 2
            for s2, cnt2 in wready_at.pop(j):
                wt(t, s2, cnt2)
            for tb in range(2):
                b_ = bb + tb
                if gr == 0 and mm_bank_war.get(b_) is not None:
                    wt(t, *mm_bank_war[b_])
                for k in range(8):
                    hh = gr * 8 + k
                    ins = t.matmul(
                        bk(b_), wgt[:, k * 128:k * 128 + 128],
                        x1nT[:, hh * TOK + tb * 512: hh * TOK + tb * 512 + 512],
                        start=(hh == 0), stop=(hh == HC - 1))
                inc("t", ins)
            qE_war[(m, gr)] = N["t"]
            if j + PRE < len(gran):
                loadE(j + PRE)

        prepE(0)
        gi = 0
        for ic in range(IC):
            for m in ("G", "U"):
                for gr in range(4):
                    j = gi
                    gi += 1
                    if j + 1 < len(gran):
                        prepE(j + 1)
                    mmE(j)
                if m == "U":
                    for tb in range(2):
                        wt(a, "t")
                        if ic >= 1:
                            wt(a, "v", sg_war[tb])
                        inc("a", a.activation(sgb[tb][:], bk(0 + tb), AF.Silu))
                        wt(v, "a")
                        if tb == 0:
                            hdst = h_sb[:, ic * 512:(ic + 1) * 512]
                        else:
                            dwt(v, spill_at.get(ic % 4))
                            hdst = hsp[ic % 4][:]
                        ins = v.tensor_tensor(hdst, sgb[tb][:], bk(2 + tb), OP.mult)
                        cnt = inc("v", ins)
                        sg_war[tb] = cnt
                        if tb == 1:
                            wt(g, "v", cnt)
                            spill_at[ic % 4] = ddma(f"hs{ic % 4}", g,
                                                    hbd[:, ic, :], hsp[ic % 4][:])
                    for b_ in range(4):
                        mm_bank_war[b_] = ("v", N["v"])
            szE_war[ic % 2] = N["t"]
            if ic + 2 < IC:
                loadE_sz(ic + 2)
        spill_done = [spill_at[k] for k in spill_at]
        sfree("hsp3", "hsp2", "hsp1", "hsp0", "sg1", "sg0", "wgt1", "wgt0",
              "dqE1", "dqE0", "szb1", "szb0",
              "qub3", "qub2", "qub1", "qub0", "qgb3", "qgb2", "qgb1", "qgb0")
        sfree("x1nT")
        fence()

        ffn_sb = salloc("ffn", [128, 32 * TOK], BF16, side="right")
        qdb = [[salloc("qdb00", [128, 1024], I32), salloc("qdb01", [128, 1024], I32)],
               [salloc("qdb10", [128, 1024], I32), salloc("qdb11", [128, 1024], I32)]]
        hbb = [salloc("hbb0", [128, 4096], BF16), salloc("hbb1", [128, 4096], BF16)]
        szdb = [salloc("szdb0", [128, 344], F32), salloc("szdb1", [128, 344], F32)]
        dqd = [salloc("dqd0", [128, 256], BF16), salloc("dqd1", [128, 256], BF16)]
        wdt = [salloc("wdt0", [128, 256], BF16), salloc("wdt1", [128, 256], BF16)]

        nblk = 11
        ldF = {}
        qF_war = {}
        hF_war = {}
        szF_war = [0, 0]

        def loadF_sz(grp):
            wt(sy, "t", szF_war[grp % 2])
            ldF[("sz", grp)] = ddma(
                f"szd{grp % 2}", sy,
                szdb[grp % 2].rearrange("p (a b m) -> p a b m", a=2, b=2),
                szd[grp].rearrange("a b p m -> p a b m"))

        def loadF(grp, bi):
            ic0 = bi * 8
            nic = min(8, IC - ic0)
            par = bi % 2
            tags = []
            for fsub in range(2):
                fc = grp * 2 + fsub
                wt(sy, "t", qF_war.get((fsub, par), 0))
                tags.append(ddma(
                    f"qd{fsub}{par}", sy, qdb[fsub][par][:, 0:nic * 128],
                    qd5[fc][:, ic0 * 128: ic0 * 128 + nic * 128]))
            wt(sy, "t", hF_war.get(par, 0))
            for sp in spill_done:
                dwt(sy, sp)
            tags.append(ddma(
                f"hb{par}", sy, hbb[par][:, 0:nic * 512],
                hbd[:, ic0:ic0 + nic, :].rearrange("p i m -> p (i m)")))
            ldF[(grp, bi)] = tags

        loadF_sz(0)
        loadF(0, 0)
        dqF_war = [0, 0]
        tpd_war = {}
        dn_bank_war = {}
        negzF = {}
        wreadyF = {}

        def prepF(grp, ic):
            bi = ic // 8
            ii = ic % 8
            par = bi % 2
            ldtag = ldF[(grp, bi)]
            dq = dqd[ic % 2]
            done = {}
            for fsub in range(2):
                eng = "v" if (ic + fsub) % 3 else "p"
                zc = szdb[grp % 2][:, fsub * 172 + 86 + ic:
                                   fsub * 172 + 86 + ic + 1]
                sc = szdb[grp % 2][:, fsub * 172 + ic: fsub * 172 + ic + 1]
                qsl = qdb[fsub][par][:, ii * 128: ii * 128 + 128]
                dsl = dq[:, fsub * 128:fsub * 128 + 128]
                if eng == "v":
                    for tg in ldtag:
                        dwt(v, tg)
                    wt(v, "v", negzF[grp])
                    wt(v, "t", dqF_war[ic % 2])
                    ins = v.tensor_scalar(dsl, qsl, zc, sc, OP.add, OP.mult)
                    done["v"] = inc("v", ins)
                else:
                    for tg in ldtag:
                        dwt(g, tg)
                    wt(g, "v", negzF[grp])
                    wt(g, "t", dqF_war[ic % 2])
                    ins = g.tensor_scalar(dsl, qsl, zc, sc, OP.add, OP.mult)
                    done["p"] = inc("p", ins)
            b_ = 4 + (ic % 2)
            if tpd_war.get(b_) is not None:
                wt(t, *tpd_war[b_])
            for s_, cnt in done.items():
                wt(t, s_, cnt)
            for fsub in range(2):
                ins = t.matmul(bk(b_, fsub * 128, 128),
                               dq[:, fsub * 128:fsub * 128 + 128],
                               ident[:], start=True, stop=True)
            dqF_war[ic % 2] = inc("t", ins)
            eng2, s2 = (v, "v") if ic % 2 == 0 else (a, "a")
            wt(eng2, "t")
            wd_ = wdt[ic % 2]
            if eng2 is v:
                ins = v.tensor_copy(wd_[:], bk(b_, 0, 256))
            else:
                ins = a.activation(wd_[:], bk(b_, 0, 256), AF.Copy)
            tpd_war[b_] = (s2, inc(s2, ins))
            wreadyF[ic] = tpd_war[b_]

        def mmF(grp, ic):
            bi = ic // 8
            ii = ic % 8
            par = bi % 2
            wd_ = wdt[ic % 2]
            wt(t, *wreadyF.pop(ic))
            for fsub in range(2):
                for th in range(2):
                    b2 = fsub * 2 + th
                    if ic == 0 and dn_bank_war.get(b2) is not None:
                        wt(t, *dn_bank_war[b2])
                    rhs = h_sb[:, ic * 512:(ic + 1) * 512] if th == 0 \
                        else hbb[par][:, ii * 512: ii * 512 + 512]
                    ins = t.matmul(bk(b2),
                                   wd_[:, fsub * 128:fsub * 128 + 128],
                                   rhs, start=(ic == 0), stop=(ic == IC - 1))
            inc("t", ins)
            for fsub in range(2):
                qF_war[(fsub, par)] = N["t"]
            hF_war[par] = N["t"]

        for grp in range(16):
            if grp + 1 < 16:
                loadF_sz(grp + 1)
            dwt(v, ldF[("sz", grp)])
            inc("v", v.tensor_scalar(szdb[grp % 2][:, 86:172],
                                     szdb[grp % 2][:, 86:172], -1.0, None,
                                     OP.mult, OP.bypass))
            ins = v.tensor_scalar(szdb[grp % 2][:, 258:344],
                                  szdb[grp % 2][:, 258:344], -1.0, None,
                                  OP.mult, OP.bypass)
            negzF[grp] = inc("v", ins)
            prepF(grp, 0)
            for ic in range(IC):
                bi = ic // 8
                if ic % 8 == 0:
                    if bi + 1 < nblk:
                        loadF(grp, bi + 1)
                    elif grp + 1 < 16:
                        pass
                if ic + 1 < IC:
                    prepF(grp, ic + 1)
                mmF(grp, ic)
                if ic == IC - 1 and grp + 1 < 16:
                    loadF(grp + 1, 0)
            szF_war[grp % 2] = N["t"]
            for fsub in range(2):
                for th in range(2):
                    b2 = fsub * 2 + th
                    fck = grp * 2 + fsub
                    wt(a, "t")
                    dst = ffn_sb[:, fck * TOK + th * 512: fck * TOK + th * 512 + 512]
                    ins = a.activation(dst, bk(b2), AF.Copy)
                    dn_bank_war[b2] = ("a", inc("a", ins))
        sfree("wdt1", "wdt0", "dqd1", "dqd0", "szdb1", "szdb0",
              "hbb1", "hbb0", "qdb11", "qdb10", "qdb01", "qdb00", "h_sb")
        fence()

        fT0 = salloc("fT0", [128, D], BF16)
        x1t = [salloc("x1t0", [128, D], F32), salloc("x1t1", [128, D], F32)]
        ldG = {}
        x1t_war = [0, 0]
        stG = {}

        def loadG(tc):
            wt(sy, "v", x1t_war[tc % 2])
            dwt(sy, stG.get(tc % 2))
            dwt(sy, x1st_at[tc])
            ldG[tc] = ddma(f"xg{tc % 2}", sy, x1t[tc % 2][:],
                           x1d[tc * 128:tc * 128 + 128, :])

        loadG(0)
        loadG(1)
        ffn3 = ffn_sb.rearrange("p (f w) -> p f w", f=32)
        fdst = fT0.rearrange("p (f w) -> p f w", f=32)
        fT_war = 0
        for tc in range(8):
            for grp in range(8):
                b_ = 4 + grp % 2
                if tp_war.get(b_) is not None:
                    wt(t, *tp_war[b_])
                wt(t, "v", fT_war)
                for k in range(4):
                    f = grp * 4 + k
                    ins = t.matmul(bk(b_, 128 * k, 128),
                                   ffn3[:, f, tc * 128:tc * 128 + 128],
                                   ident[:], start=True, stop=True)
                inc("t", ins)
                eng, s_ = (v, "v") if grp % 2 == 0 else (a, "a")
                wt(eng, "t")
                pv = bk(b_).rearrange("p (k m) -> p k m", k=4)
                if eng is v:
                    ins = v.tensor_copy(fdst[:, grp * 4:grp * 4 + 4, :], pv)
                else:
                    ins = a.activation(fdst[:, grp * 4:grp * 4 + 4, :], pv, AF.Copy)
                tp_war[b_] = (s_, inc(s_, ins))
            wt(v, "a")
            wt(v, "v")
            ln_prep(fT0)
            dwt(v, ldG[tc])
            x_ = x1t[tc % 2]
            inc("v", v.tensor_scalar(x_[:], x_[:], nmr1[:], None, OP.add, OP.bypass))
            ins = v.scalar_tensor_tensor(x_[:], fT0[:], r1[:], x_[:],
                                         OP.mult, OP.add)
            fT_war = inc("v", ins)
            x1t_war[tc % 2] = N["v"]
            wt(g, "v")
            stG[tc % 2] = ddma(f"os{tc % 2}", g,
                               outd[tc * 128:tc * 128 + 128, :], x_[:])
            if tc + 2 < 8:
                loadG(tc + 2)
        sfree("x1t1", "x1t0", "fT0")
        for nm in sorted(dslots):
            sy.wait_ge(dslots[nm][0], dslots[nm][1])

    @blk.sync
    def _(_sync):
        emit()

    st.close()
    return nc


def _prep_inputs(inp):
    f32 = np.float32
    x = np.asarray(inp["L0_attn_x"], f32)
    wq = np.asarray(inp["L0_attn_wq"], f32)
    wk = np.asarray(inp["L0_attn_wk"], f32)
    wv = np.asarray(inp["L0_attn_wv"], f32)
    wo = np.asarray(inp["L0_attn_wo"], f32)
    qg = np.asarray(inp["L0_ffn_qw_gate"], np.int32)
    qu = np.asarray(inp["L0_ffn_qw_up"], np.int32)
    qd = np.asarray(inp["L0_ffn_qw_down"], np.int32)
    sg = np.asarray(inp["L0_ffn_s_gate"], f32)
    zg = np.asarray(inp["L0_ffn_z_gate"], f32)
    su = np.asarray(inp["L0_ffn_s_up"], f32)
    zu = np.asarray(inp["L0_ffn_z_up"], f32)
    sd = np.asarray(inp["L0_ffn_s_down"], f32)
    zd = np.asarray(inp["L0_ffn_z_down"], f32)

    def tile4(wT, nf):
        return np.ascontiguousarray(
            wT.reshape(HC, 128, nf, 128).transpose(2, 1, 0, 3))

    wq4 = tile4(wq.T, NQ)
    wk4 = tile4(wk.T, NKV)
    wv4 = tile4(wv.T, NKV)
    wo4 = np.ascontiguousarray(wo.T.reshape(32, 128, 8, 512).transpose(2, 0, 1, 3))
    qg5 = np.ascontiguousarray(qg.reshape(IC, 128, D))
    qu5 = np.ascontiguousarray(qu.reshape(IC, 128, D))
    qd5 = np.ascontiguousarray(qd.reshape(32, 128, II))
    szgu = np.ascontiguousarray(
        np.stack([sg.reshape(IC, 128, 32), zg.reshape(IC, 128, 32),
                  su.reshape(IC, 128, 32), zu.reshape(IC, 128, 32)], axis=1))
    sd5 = sd.reshape(16, 2, 128, IC)
    zd5 = zd.reshape(16, 2, 128, IC)
    szd_ = np.ascontiguousarray(np.stack([sd5, zd5], axis=2))

    ident = np.eye(128, dtype=ml_dtypes.bfloat16)
    jj = np.arange(640)[None, :]
    pp = np.arange(128)[:, None]
    base = (jj > pp) & (jj <= pp + 512)
    m_even = np.stack([(base & (jj >= 512 - vv * 128)) if vv < 4 else base
                       for vv in range(5)]).astype(ml_dtypes.bfloat16)
    m_odd = np.stack([base] * 5).astype(ml_dtypes.bfloat16)

    in_maps = []
    for core in range(8):
        b, half = divmod(core, 2)
        q0 = half * TOK
        xmain = np.ascontiguousarray(x[b, q0:q0 + TOK])
        xhalo = (np.ascontiguousarray(x[b, q0 - HALO:q0]) if half == 1
                 else np.zeros((HALO, D), f32))
        in_maps.append({
            "xm": xmain, "xh": xhalo,
            "wq4": wq4, "wk4": wk4, "wv4": wv4, "wo4": wo4,
            "qg5": qg5, "qu5": qu5, "qd5": qd5,
            "szgu": szgu, "szd": szd_,
            "maskd": m_odd if half == 1 else m_even,
            "identd": ident,
        })
    return in_maps


def kernel(**inputs):
    in_maps = _prep_inputs(inputs)
    nc = build_core_program()
    res = run_bass_kernel_spmd(nc, in_maps, core_ids=list(range(8)))
    out = np.empty((4, 2048, D), np.float32)
    for core in range(8):
        b, half = divmod(core, 2)
        out[b, half * TOK:(half + 1) * TOK] = res.results[core]["outd"]
    return out


# version 07 (final)
# speedup vs baseline: 1.3914x; 1.0366x over previous
import numpy as np
import ml_dtypes
import contextlib
import concourse.bass as bass
import concourse.mybir as mybir
from concourse.alu_op_type import AluOpType
from concourse.bass_utils import run_bass_kernel_spmd

F32 = mybir.dt.float32
BF16 = mybir.dt.bfloat16
I32 = mybir.dt.int32
AF = mybir.ActivationFunctionType
OP = AluOpType

TOK = 1024
HALO = 512
D = 4096
HC = 32
NQ, NKV, HD = 32, 8, 128
II = 11008
IC = 86
SCALE = float(1.0 / np.sqrt(128.0))
EPS = 1e-5
KEYS = TOK + HALO
KC = 12


def build_core_program():
    nc = bass.Bass("TRN2", target_bir_lowering=False)

    xm = nc.dram_tensor("xm", [TOK, D], F32, kind="ExternalInput").ap()
    xh = nc.dram_tensor("xh", [HALO, D], F32, kind="ExternalInput").ap()
    wq4 = nc.dram_tensor("wq4", [NQ, 128, HC, 128], F32, kind="ExternalInput").ap()
    wk4 = nc.dram_tensor("wk4", [NKV, 128, HC, 128], F32, kind="ExternalInput").ap()
    wv4 = nc.dram_tensor("wv4", [NKV, 128, HC, 128], F32, kind="ExternalInput").ap()
    wo4 = nc.dram_tensor("wo4", [8, 32, 128, 512], F32, kind="ExternalInput").ap()
    qg5 = nc.dram_tensor("qg5", [IC, 128, D], I32, kind="ExternalInput").ap()
    qu5 = nc.dram_tensor("qu5", [IC, 128, D], I32, kind="ExternalInput").ap()
    qd5 = nc.dram_tensor("qd5", [32, 128, II], I32, kind="ExternalInput").ap()
    szgu = nc.dram_tensor("szgu", [IC, 4, 128, 32], F32, kind="ExternalInput").ap()
    szd = nc.dram_tensor("szd", [16, 2, 2, 128, IC], F32, kind="ExternalInput").ap()
    maskd = nc.dram_tensor("maskd", [5, 128, 640], BF16, kind="ExternalInput").ap()
    identd = nc.dram_tensor("identd", [128, 128], BF16, kind="ExternalInput").ap()
    outd = nc.dram_tensor("outd", [TOK, D], F32, kind="ExternalOutput").ap()
    x1d = nc.dram_tensor("x1d", [TOK, D], F32).ap()
    hbd = nc.dram_tensor("hbd", [128, IC, 512], BF16).ap()

    st = contextlib.ExitStack()

    SEMS = ["t", "v", "a", "p"]
    H = {s: st.enter_context(nc.semaphore(s)) for s in SEMS}
    N = {s: 0 for s in SEMS}

    def inc(s, instr, k=1):
        instr.then_inc(H[s], k)
        N[s] += k
        return N[s]

    def wt(eng, s, target=None):
        if target is None:
            target = N[s]
        if target > 0:
            eng.wait_ge(H[s], target)

    dslots = {}

    def ddma(name, qeng, dst, src_):
        if name not in dslots:
            dslots[name] = [st.enter_context(nc.semaphore("D" + name)), 0]
        ent = dslots[name]
        qeng.dma_start(dst, src_).then_inc(ent[0], 16)
        ent[1] += 16
        return (name, ent[1])

    def dwt(eng, tag):
        if tag is None:
            return
        nm, target = tag
        eng.wait_ge(dslots[nm][0], target)

    guards = {}

    def salloc(name, shape, dt, side="left"):
        gd = nc.sbuf_tensor(name, shape, dt, side=side)
        h = gd.__enter__()
        guards[name] = gd
        return h

    def sfree(*names):
        for name in names:
            guards.pop(name).__exit__(None, None, None)

    psum = st.enter_context(nc.psum_tensor("ps", [128, 4096], F32))

    def bk(k, off=0, n=512):
        return psum[:, 512 * k + off: 512 * k + off + n]

    ident = salloc("ident", [128, 128], BF16)
    eps_sb = salloc("eps", [128, 1], F32)
    stat6 = salloc("stat6", [128, 48], F32)
    mv2 = salloc("mv2", [128, 2], F32)
    std1 = salloc("std1", [128, 1], F32)
    r1 = salloc("r1", [128, 1], F32)
    nmr1 = salloc("nmr1", [128, 1], F32)
    wos1 = salloc("wos1", [128, 64], F32)
    wos2 = salloc("wos2", [128, 64], F32)
    r2a = salloc("r2a", [128, 8], F32)
    nmr2a = salloc("nmr2a", [128, 8], F32)
    sum1 = salloc("sum1", [128, 1], F32)
    sum2 = salloc("sum2", [128, 1], F32)
    mu1 = salloc("mu1", [128, 1], F32)
    nzs = salloc("nzs", [128, 64], F32)

    v, a, t, g, sy = nc.vector, nc.scalar, nc.tensor, nc.gpsimd, nc.sync

    blk = st.enter_context(nc.Block())

    def fence():
        for eng in (t, v, a, g, sy):
            for s in ("v", "a", "t", "p"):
                wt(eng, s)

    def emit():
        cst_tag = ddma("cst", sy, ident[:], identd)
        dwt(v, cst_tag)
        v.memset(eps_sb[:], EPS)
        inc("v", v.tensor_scalar(eps_sb[:], eps_sb[:], 1.0, None, OP.mult, OP.bypass))

        def ln_prep(src):
            for c in range(8):
                ins = v.bn_stats(stat6[:, 6 * c:6 * c + 6],
                                 src[:, 512 * c:512 * c + 512])
            inc("v", ins)
            wt(v, "v")
            inc("v", v.bn_aggr(mv2[:], stat6[:, 0:48]))
            wt(a, "v")
            inc("a", a.activation(std1[:], mv2[:, 1:2], AF.Sqrt,
                                  bias=eps_sb[:], scale=1.0))
            wt(v, "a")
            inc("v", v.reciprocal(r1[:], std1[:]))
            wt(v, "v")
            inc("v", v.scalar_tensor_tensor(nmr1[:], mv2[:, 0:1], -1.0, r1[:],
                                            OP.mult, OP.mult))
            wt(v, "v")

        tp_war = {}

        def transpose_tile(src, width, dst3):
            nh = width // 128
            for grp in range(nh // 4):
                b_ = 4 + grp % 2
                if tp_war.get(b_) is not None:
                    wt(t, *tp_war[b_])
                for k in range(4):
                    hh = grp * 4 + k
                    ins = t.matmul(bk(b_, 128 * k, 128),
                                   src[:, 128 * hh:128 * hh + 128], ident[:],
                                   start=True, stop=True)
                inc("t", ins)
                eng, s_ = (v, "v") if grp % 2 == 0 else (a, "a")
                wt(eng, "t")
                pv = bk(b_).rearrange("p (k m) -> p k m", k=4)
                if eng is v:
                    ins = v.tensor_copy(dst3(grp * 4, 4), pv)
                else:
                    ins = a.activation(dst3(grp * 4, 4), pv, AF.Copy)
                tp_war[b_] = (s_, inc(s_, ins))

        xnT_m = salloc("xnT_m", [128, HC * TOK], BF16, side="right")
        xnT_h = salloc("xnT_h", [128, HC * HALO], BF16, side="right")
        xb = [salloc("xbA0", [128, D], F32, side="right"),
              salloc("xbA1", [128, D], F32, side="right")]
        xt = [salloc("xtA0", [128, D], BF16, side="right"),
              salloc("xtA1", [128, D], BF16, side="right")]
        tiles = [("m", i) for i in range(8)] + [("h", i) for i in range(4)]
        ld_at = {}
        xb_war = [0, 0]
        xt_war = [0, 0]

        def loadA(i):
            kind, tc = tiles[i]
            src = xm if kind == "m" else xh
            wt(sy, "v", xb_war[i % 2])
            ld_at[i] = ddma(f"xa{i % 2}", sy, xb[i % 2][:],
                            src[tc * 128:tc * 128 + 128, :])

        loadA(0)
        loadA(1)
        dm = xnT_m.rearrange("p (h w) -> p h w", h=HC)
        dh = xnT_h.rearrange("p (h w) -> p h w", h=HC)
        for i, (kind, tc) in enumerate(tiles):
            dwt(v, ld_at[i])
            ln_prep(xb[i % 2])
            wt(v, "t", xt_war[i % 2])
            ins = v.tensor_scalar(xt[i % 2][:], xb[i % 2][:], r1[:], nmr1[:],
                                  OP.mult, OP.add)
            xb_war[i % 2] = inc("v", ins)
            wt(t, "v")
            dst = dm if kind == "m" else dh

            def dst3(h0, n, tc=tc, dst=dst):
                return dst[:, h0:h0 + n, tc * 128:tc * 128 + 128]
            transpose_tile(xt[i % 2], D, dst3)
            xt_war[i % 2] = N["t"]
            if i + 2 < len(tiles):
                loadA(i + 2)
        sfree("xtA1", "xtA0", "xbA1", "xbA0")
        fence()

        K_sb = salloc("K", [128, NKV * KEYS], BF16)
        VT_sb = salloc("VT", [128, KC * NKV * 128], BF16)

        def proj_rhs(hh, tb):
            if tb == 0:
                return xnT_h[:, hh * HALO: hh * HALO + 512]
            o = hh * TOK + (tb - 1) * 512
            return xnT_m[:, o:o + 512]

        def do_proj(units, wbuf, ntb, store, pfx):
            wb_war = [0, 0]
            wd_at = {}
            pj_war = {}

            def loadW(idx):
                src4, fo = units[idx]
                wt(g, "t", wb_war[idx % 2])
                wd_at[idx] = ddma(f"{pfx}{idx % 2}", g, wbuf[idx % 2][:],
                                  src4[fo].rearrange("p h b -> p (h b)"))

            loadW(0)
            if len(units) > 1:
                loadW(1)
            for idx in range(len(units)):
                dwt(t, wd_at[idx])
                for tb in range(ntb):
                    b_ = tb % 2
                    if pj_war.get(b_) is not None:
                        wt(t, *pj_war[b_])
                    for hh in range(HC):
                        ins = t.matmul(bk(b_),
                                       wbuf[idx % 2][:, hh * 128:hh * 128 + 128],
                                       proj_rhs(hh, tb + (3 - ntb)),
                                       start=(hh == 0), stop=(hh == HC - 1))
                    inc("t", ins)
                    eng, s_ = (v, "v") if tb % 2 == 0 else (a, "a")
                    wt(eng, "t")
                    dstap = store(idx, tb)
                    if eng is v:
                        ins = v.tensor_copy(dstap, bk(b_))
                    else:
                        ins = a.activation(dstap, bk(b_), AF.Copy)
                    pj_war[b_] = (s_, inc(s_, ins))
                wb_war[idx % 2] = N["t"]
                if idx + 2 < len(units):
                    loadW(idx + 2)
                yield idx, pj_war

        wbB = [salloc("wbB0", [128, D], BF16), salloc("wbB1", [128, D], BF16)]
        vfm = salloc("vfm", [128, KEYS], BF16)
        unitsB1 = [(wk4, i) for i in range(NKV)] + [(wv4, i) for i in range(NKV)]

        def storeB1(idx, tb):
            if idx < NKV:
                o = idx * KEYS + tb * 512
                return K_sb[:, o:o + 512]
            return vfm[:, tb * 512: tb * 512 + 512]

        dVT = VT_sb.rearrange("p (c e m) -> p c e m", c=KC, e=NKV)
        for idx, pj_war in do_proj(unitsB1, wbB, 3, storeB1, "wb"):
            if idx >= NKV:
                fo = idx - NKV
                for b_ in (0, 1):
                    if pj_war.get(b_) is not None:
                        wt(t, *pj_war[b_])

                def dst3(tc0, n, fo=fo):
                    return dVT[:, tc0:tc0 + n, fo, :]
                transpose_tile(vfm, KEYS, dst3)
        sfree("vfm", "wbB1", "wbB0")
        sfree("xnT_h")
        fence()

        Q_sb = salloc("Q", [128, NQ * TOK], BF16)
        wbQ = [salloc("wbQ0", [128, D], BF16), salloc("wbQ1", [128, D], BF16)]
        unitsB2 = [(wq4, i) for i in range(NQ)]

        def storeB2(idx, tb):
            o = idx * TOK + tb * 512
            return Q_sb[:, o:o + 512]

        for idx, pj_war in do_proj(unitsB2, wbQ, 2, storeB2, "wb"):
            pass
        sfree("wbQ1", "wbQ0")
        sfree("xnT_m")
        fence()

        att_sb = salloc("att", [128, NQ * TOK], BF16, side="right")
        masks = salloc("masks", [128, 5 * 640], BF16)
        junk = salloc("junk", [128, 640], BF16)
        pexp = [salloc("pexp0", [128, 640], BF16), salloc("pexp1", [128, 640], BF16)]
        pms = [salloc("pm0", [128, 640], BF16), salloc("pm1", [128, 640], BF16)]
        pts = [salloc("pt0", [128, 640], BF16), salloc("pt1", [128, 640], BF16)]
        bias = [salloc("bias0", [128, 1], F32), salloc("bias1", [128, 1], F32)]
        rsum = [salloc("rs0", [128, 1], F32), salloc("rs1", [128, 1], F32)]
        rnrm = [salloc("rn0", [128, 1], F32), salloc("rn1", [128, 1], F32)]
        dwt(sy, cst_tag)
        cst_tag2 = ddma("cst", sy, masks.rearrange("p (v m) -> p v m", v=5),
                        maskd.rearrange("v p m -> p v m"))
        dwt(v, cst_tag2)
        mask3 = masks.rearrange("p (m w) -> p m w", m=5)

        last_ptevac = [None, None]
        last_attevac = None
        smdone = {}

        def smC(u):
            qh, qc = divmod(u, 8)
            kvh = qh // 4
            par = u % 2
            so = 1024 * par
            if last_ptevac[par] is not None:
                wt(t, *last_ptevac[par])
            lhs = Q_sb[:, qh * TOK + qc * 128: qh * TOK + qc * 128 + 128]
            kb = kvh * KEYS + qc * 128
            t.matmul(psum[:, so:so + 512], lhs, K_sb[:, kb:kb + 512],
                     start=True, stop=True)
            ins = t.matmul(psum[:, so + 512:so + 640], lhs,
                           K_sb[:, kb + 512:kb + 640], start=True, stop=True)
            inc("t", ins)
            wt(v, "t")
            ins = v.tensor_scalar(junk[:], psum[:, so:so + 640], -SCALE, None,
                                  OP.mult, OP.min, accum_out=bias[par][:])
            inc("v", ins)
            wt(a, "v")
            ins = a.activation(pexp[par][:], psum[:, so:so + 640], AF.Exp,
                               bias=bias[par][:], scale=SCALE)
            inc("a", ins)
            mvv = min(qc, 4)
            wt(v, "a")
            ins = v.scalar_tensor_tensor(pms[par][:], pexp[par][:], 1.0,
                                         mask3[:, mvv, :], OP.mult, OP.mult,
                                         accum_out=rsum[par][:])
            inc("v", ins)
            wt(v, "v")
            inc("v", v.reciprocal(rnrm[par][:], rsum[par][:]))
            wt(v, "v")
            ins = v.tensor_scalar(pexp[par][:], pms[par][:], rnrm[par][:],
                                  None, OP.mult, OP.bypass)
            smdone[u] = inc("v", ins)

        def pvC(u):
            qh, qc = divmod(u, 8)
            kvh = qh // 4
            par = u % 2
            so = 1024 * par
            nonlocal last_attevac
            wt(t, "v", smdone.pop(u))
            for j in range(5):
                ins = t.matmul(psum[:, so + 128 * j: so + 128 * j + 128],
                               pexp[par][:, j * 128:j * 128 + 128], ident[:],
                               start=True, stop=True)
            inc("t", ins)
            wt(v, "t")
            ins = v.tensor_copy(pts[par][:], psum[:, so:so + 640])
            last_ptevac[par] = ("v", inc("v", ins))
            wt(t, *last_ptevac[par])
            if u % 4 == 0 and last_attevac is not None:
                wt(t, *last_attevac)
            for j in range(5):
                tc = qc + j
                ins = t.matmul(bk(4, (u % 4) * 128, 128),
                               VT_sb[:, (tc * NKV + kvh) * 128:
                                     (tc * NKV + kvh) * 128 + 128],
                               pts[par][:, j * 128:j * 128 + 128],
                               start=(j == 0), stop=(j == 4))
            inc("t", ins)
            if u % 4 == 3:
                wt(a, "t")
                dst = att_sb[:, qh * TOK + (qc - 3) * 128:
                             qh * TOK + (qc - 3) * 128 + 512]
                ins = a.activation(dst, bk(4), AF.Copy)
                last_attevac = ("a", inc("a", ins))

        smC(0)
        for u in range(NQ * 8):
            if u + 1 < NQ * 8:
                smC(u + 1)
            pvC(u)
        sfree("rn1", "rn0", "rs1", "rs0", "bias1", "bias0", "pt1", "pt0",
              "pm1", "pm0", "pexp1", "pexp0", "junk", "masks")
        sfree("Q", "VT", "K")
        fence()

        wo_out = salloc("wo_out", [128, 8 * D], BF16)
        sqd = salloc("sqd", [128, 512], BF16)
        wors = [salloc(f"wor{i}", [128, 16 * 512], BF16) for i in range(3)]
        wor_war = [0, 0, 0]
        wd_half = {}

        def load_wo(hi):
            fg, hf = divmod(hi, 2)
            ring = hi % 3
            wt(g, "t", wor_war[ring])
            wd_half[hi] = ddma(
                f"wo{ring}", g, wors[ring].rearrange("p (h b) -> p h b", h=16),
                wo4[fg][16 * hf:16 * hf + 16].rearrange("h p b -> p h b"))

        load_wo(0)
        load_wo(1)
        wo_bank_war = {}
        for fg in range(8):
            if 2 * fg + 2 < 16:
                load_wo(2 * fg + 2)
            dwt(t, wd_half[2 * fg])
            dwt(t, wd_half[2 * fg + 1])
            for tc in range(8):
                b_ = tc % 2
                if wo_bank_war.get(b_) is not None:
                    wt(t, *wo_bank_war[b_])
                for ac in range(32):
                    ring = (2 * fg + ac // 16) % 3
                    ins = t.matmul(
                        bk(b_),
                        att_sb[:, ac * TOK + tc * 128: ac * TOK + tc * 128 + 128],
                        wors[ring][:, (ac % 16) * 512:(ac % 16) * 512 + 512],
                        start=(ac == 0), stop=(ac == 31))
                inc("t", ins)
                wt(a, "t")
                dst = wo_out[:, tc * D + fg * 512: tc * D + fg * 512 + 512]
                inc("a", a.activation(dst, bk(b_), AF.Copy,
                                      accum_out=wos1[:, tc * 8 + fg: tc * 8 + fg + 1]))
                ins = a.activation(sqd[:], bk(b_), AF.Square,
                                   accum_out=wos2[:, tc * 8 + fg: tc * 8 + fg + 1])
                wo_bank_war[b_] = ("a", inc("a", ins))
            wor_war[(2 * fg) % 3] = N["t"]
            wor_war[(2 * fg + 1) % 3] = N["t"]
            if 2 * fg + 3 < 16:
                load_wo(2 * fg + 3)
        inc("a", a.activation(sqd[:, 0:1], eps_sb[:], AF.Copy))
        sfree("wor2", "wor1", "wor0", "sqd")
        sfree("att")
        fence()

        x1nT = salloc("x1nT", [128, HC * TOK], BF16, side="right")
        xbD = [salloc("xbD0", [128, D], F32), salloc("xbD1", [128, D], F32)]
        x1n0 = salloc("x1nD", [128, D], BF16)
        wt(v, "a")
        for tc in range(8):
            inc("v", v.tensor_reduce(sum1[:], wos1[:, tc * 8:tc * 8 + 8],
                                     axis=mybir.AxisListType.X, op=OP.add))
            inc("v", v.tensor_reduce(sum2[:], wos2[:, tc * 8:tc * 8 + 8],
                                     axis=mybir.AxisListType.X, op=OP.add))
            wt(v, "v")
            inc("v", v.tensor_scalar(mu1[:], sum1[:], 1.0 / D, None,
                                     OP.mult, OP.bypass))
            wt(v, "v")
            inc("v", v.scalar_tensor_tensor(std1[:], mu1[:], 1.0, mu1[:],
                                            OP.mult, OP.mult))
            wt(v, "v")
            inc("v", v.tensor_scalar(mv2[:, 1:2], sum2[:], 1.0 / D, std1[:],
                                     OP.mult, OP.subtract))
            wt(a, "v")
            inc("a", a.activation(std1[:], mv2[:, 1:2], AF.Sqrt,
                                  bias=eps_sb[:], scale=1.0))
            wt(v, "a")
            inc("v", v.reciprocal(r2a[:, tc:tc + 1], std1[:]))
            wt(v, "v")
            inc("v", v.scalar_tensor_tensor(nmr2a[:, tc:tc + 1], mu1[:], -1.0,
                                            r2a[:, tc:tc + 1], OP.mult, OP.mult))
            wt(v, "v")

        ldD = {}
        xbD_war = [0, 0]
        stD = {}
        x1st_at = []

        def loadD(tc):
            wt(sy, "v", xbD_war[tc % 2])
            dwt(sy, stD.get(tc % 2))
            ldD[tc] = ddma(f"xd{tc % 2}", sy, xbD[tc % 2][:],
                           xm[tc * 128:tc * 128 + 128, :])

        loadD(0)
        loadD(1)
        dx1 = x1nT.rearrange("p (h w) -> p h w", h=HC)
        for tc in range(8):
            x_ = xbD[tc % 2]
            dwt(v, ldD[tc])
            inc("v", v.tensor_scalar(x_[:], x_[:], nmr2a[:, tc:tc + 1], None,
                                     OP.add, OP.bypass))
            ins = v.scalar_tensor_tensor(x_[:], wo_out[:, tc * D:(tc + 1) * D],
                                         r2a[:, tc:tc + 1], x_[:], OP.mult, OP.add)
            inc("v", ins)
            wt(g, "v")
            stD[tc % 2] = ddma(f"xs{tc % 2}", g,
                               x1d[tc * 128:tc * 128 + 128, :], x_[:])
            x1st_at.append(stD[tc % 2])
            ln_prep(x_)
            inc("v", v.tensor_scalar(x1n0[:], x_[:], r1[:], nmr1[:],
                                     OP.mult, OP.add))
            wt(t, "v")

            def dst3(h0, n, tc=tc):
                return dx1[:, h0:h0 + n, tc * 128:tc * 128 + 128]
            transpose_tile(x1n0, D, dst3)
            xbD_war[tc % 2] = N["v"]
            wt(v, "t")
            if tc + 2 < 8:
                loadD(tc + 2)
        sfree("x1nD", "xbD1", "xbD0", "wo_out")
        fence()

        h_sb = salloc("h_sb", [128, IC * 512], BF16)
        qb = {"G": [salloc(f"qgb{i}", [128, 1024], I32) for i in range(4)],
              "U": [salloc(f"qub{i}", [128, 1024], I32) for i in range(4)]}
        szb = [salloc("szb0", [128, 128], F32), salloc("szb1", [128, 128], F32)]
        dqs = [salloc("dqE0", [128, 1024], BF16), salloc("dqE1", [128, 1024], BF16)]
        wgts = [salloc("wgt0", [128, 1024], BF16), salloc("wgt1", [128, 1024], BF16)]
        sgb = [salloc("sg0", [128, 512], BF16), salloc("sg1", [128, 512], BF16)]
        hsp = [salloc(f"hsp{i}", [128, 512], BF16) for i in range(4)]

        deq_eng = []
        for k in range(16):
            deq_eng.append("v" if k < 9 else ("a" if k < 13 else "p"))
        deq_idx = 0

        gran = [(ic, m, gr) for ic in range(IC) for m in ("G", "U")
                for gr in range(4)]
        ldE = {}
        qE_war = {}
        szE_war = [0, 0]

        def loadE_sz(ic):
            wt(sy, "t", szE_war[ic % 2])
            ldE[("sz", ic)] = ddma(
                f"sz{ic % 2}", sy, szb[ic % 2].rearrange("p (k m) -> p k m", k=4),
                szgu[ic].rearrange("k p m -> p k m"))

        def loadE(gi):
            ic, m, gr = gran[gi]
            wt(sy, "t", qE_war.get((m, gr), 0))
            src = qg5 if m == "G" else qu5
            ldE[gi] = ddma(f"q{m}{gr}", sy, qb[m][gr][:],
                           src[ic][:, gr * 1024:gr * 1024 + 1024])

        PRE = 8
        loadE_sz(0)
        loadE_sz(1)
        for j in range(PRE):
            loadE(j)

        negz_at = {}
        nzs_at = {}
        sg_war = [0, 0]
        dq_war = [0, 0]
        mm_bank_war = {}
        spill_at = {}
        wready_at = {}
        deq_state = {"idx": deq_idx}

        def prepE(j):
            ic, m, gr = gran[j]
            scol = szb[ic % 2][:, (0 if m == "G" else 64):
                               (0 if m == "G" else 64) + 32]
            zcol = szb[ic % 2][:, (32 if m == "G" else 96):
                               (32 if m == "G" else 96) + 32]
            if gr == 0:
                dwt(v, ldE[("sz", ic)])
                inc("v", v.tensor_scalar(zcol, zcol, -1.0, None,
                                         OP.mult, OP.bypass))
                negz_at[(ic, m)] = N["v"]
                wt(v, "v")
                wt(v, "a")
                ins = v.scalar_tensor_tensor(nzs[:, (0 if m == "G" else 32):
                                             (0 if m == "G" else 32) + 32],
                                             zcol, 1.0, scol, OP.mult, OP.mult)
                nzs_at[(ic, m)] = inc("v", ins)
            qbuf = qb[m][gr]
            dq = dqs[gr % 2]
            wgt = wgts[gr % 2]
            ldtag = ldE[j]
            done = {}
            for k in range(8):
                hh = gr * 8 + k
                eng = deq_eng[deq_state["idx"] % 16]
                deq_state["idx"] += 1
                sl = slice(k * 128, k * 128 + 128)
                zc = zcol[:, hh:hh + 1]
                sc = scol[:, hh:hh + 1]
                if eng == "v":
                    dwt(v, ldtag)
                    wt(v, "v", negz_at[(ic, m)])
                    wt(v, "t", dq_war[gr % 2])
                    ins = v.tensor_scalar(dq[:, sl], qbuf[:, sl], zc, sc,
                                          OP.add, OP.mult)
                    done["v"] = inc("v", ins)
                elif eng == "a":
                    dwt(a, ldtag)
                    wt(a, "v", nzs_at[(ic, m)])
                    wt(a, "t", dq_war[gr % 2])
                    nb = nzs[:, (0 if m == "G" else 32) + hh:
                             (0 if m == "G" else 32) + hh + 1]
                    ins = a.activation(dq[:, sl], qbuf[:, sl], AF.Identity,
                                       scale=sc, bias=nb)
                    done["a"] = inc("a", ins)
                else:
                    dwt(g, ldtag)
                    wt(g, "v", negz_at[(ic, m)])
                    wt(g, "t", dq_war[gr % 2])
                    ins = g.tensor_scalar(dq[:, sl], qbuf[:, sl], zc, sc,
                                          OP.add, OP.mult)
                    done["p"] = inc("p", ins)
            wready = []
            for half in range(2):
                b_ = 4 + 2 * (j % 2) + half
                if tp_war.get(b_) is not None:
                    wt(t, *tp_war[b_])
                for s_, cnt in done.items():
                    wt(t, s_, cnt)
                for k in range(4):
                    kk = half * 4 + k
                    ins = t.matmul(bk(b_, 128 * k, 128),
                                   dq[:, kk * 128:kk * 128 + 128],
                                   ident[:], start=True, stop=True)
                inc("t", ins)
                eng2, s2 = (v, "v") if half == 0 else (a, "a")
                wt(eng2, "t")
                dst = wgt[:, half * 512:half * 512 + 512]
                if eng2 is v:
                    ins = v.tensor_copy(dst, bk(b_))
                else:
                    ins = a.activation(dst, bk(b_), AF.Copy)
                cnt2 = inc(s2, ins)
                tp_war[b_] = (s2, cnt2)
                wready.append((s2, cnt2))
            dq_war[gr % 2] = N["t"]
            wready_at[j] = wready

        def mmE(j):
            ic, m, gr = gran[j]
            wgt = wgts[gr % 2]
            bb = 0 if m == "G" else 2
            for s2, cnt2 in wready_at.pop(j):
                wt(t, s2, cnt2)
            for tb in range(2):
                b_ = bb + tb
                if gr == 0 and mm_bank_war.get(b_) is not None:
                    wt(t, *mm_bank_war[b_])
                for k in range(8):
                    hh = gr * 8 + k
                    ins = t.matmul(
                        bk(b_), wgt[:, k * 128:k * 128 + 128],
                        x1nT[:, hh * TOK + tb * 512: hh * TOK + tb * 512 + 512],
                        start=(hh == 0), stop=(hh == HC - 1))
                inc("t", ins)
            qE_war[(m, gr)] = N["t"]
            if j + PRE < len(gran):
                loadE(j + PRE)

        prepE(0)
        gi = 0
        for ic in range(IC):
            for m in ("G", "U"):
                for gr in range(4):
                    j = gi
                    gi += 1
                    if j + 1 < len(gran):
                        prepE(j + 1)
                    mmE(j)
                if m == "U":
                    for tb in range(2):
                        wt(a, "t")
                        if ic >= 1:
                            wt(a, "v", sg_war[tb])
                        inc("a", a.activation(sgb[tb][:], bk(0 + tb), AF.Silu))
                        wt(v, "a")
                        if tb == 0:
                            hdst = h_sb[:, ic * 512:(ic + 1) * 512]
                        else:
                            dwt(v, spill_at.get(ic % 4))
                            hdst = hsp[ic % 4][:]
                        ins = v.tensor_tensor(hdst, sgb[tb][:], bk(2 + tb), OP.mult)
                        cnt = inc("v", ins)
                        sg_war[tb] = cnt
                        if tb == 1:
                            wt(g, "v", cnt)
                            spill_at[ic % 4] = ddma(f"hs{ic % 4}", g,
                                                    hbd[:, ic, :], hsp[ic % 4][:])
                    for b_ in range(4):
                        mm_bank_war[b_] = ("v", N["v"])
            szE_war[ic % 2] = N["t"]
            if ic + 2 < IC:
                loadE_sz(ic + 2)
        spill_done = [spill_at[k] for k in spill_at]
        sfree("hsp3", "hsp2", "hsp1", "hsp0", "sg1", "sg0", "wgt1", "wgt0",
              "dqE1", "dqE0", "szb1", "szb0",
              "qub3", "qub2", "qub1", "qub0", "qgb3", "qgb2", "qgb1", "qgb0")
        sfree("x1nT")
        fence()

        ffn_sb = salloc("ffn", [128, 32 * TOK], BF16, side="right")
        qdb = [[salloc("qdb00", [128, 1024], I32), salloc("qdb01", [128, 1024], I32)],
               [salloc("qdb10", [128, 1024], I32), salloc("qdb11", [128, 1024], I32)]]
        hbb = [salloc("hbb0", [128, 4096], BF16), salloc("hbb1", [128, 4096], BF16)]
        szdb = [salloc("szdb0", [128, 344], F32), salloc("szdb1", [128, 344], F32)]
        dqd = [salloc("dqd0", [128, 256], BF16), salloc("dqd1", [128, 256], BF16)]
        wdt = [salloc("wdt0", [128, 256], BF16), salloc("wdt1", [128, 256], BF16)]

        nblk = 11
        ldF = {}
        qF_war = {}
        hF_war = {}
        szF_war = [0, 0]

        def loadF_sz(grp):
            wt(sy, "t", szF_war[grp % 2])
            ldF[("sz", grp)] = ddma(
                f"szd{grp % 2}", sy,
                szdb[grp % 2].rearrange("p (a b m) -> p a b m", a=2, b=2),
                szd[grp].rearrange("a b p m -> p a b m"))

        def loadF(grp, bi):
            ic0 = bi * 8
            nic = min(8, IC - ic0)
            par = bi % 2
            tags = []
            for fsub in range(2):
                fc = grp * 2 + fsub
                wt(sy, "t", qF_war.get((fsub, par), 0))
                tags.append(ddma(
                    f"qd{fsub}{par}", sy, qdb[fsub][par][:, 0:nic * 128],
                    qd5[fc][:, ic0 * 128: ic0 * 128 + nic * 128]))
            wt(sy, "t", hF_war.get(par, 0))
            for sp in spill_done:
                dwt(sy, sp)
            tags.append(ddma(
                f"hb{par}", sy, hbb[par][:, 0:nic * 512],
                hbd[:, ic0:ic0 + nic, :].rearrange("p i m -> p (i m)")))
            ldF[(grp, bi)] = tags

        loadF_sz(0)
        loadF(0, 0)
        dqF_war = [0, 0]
        tpd_war = {}
        dn_bank_war = {}
        negzF = {}
        wreadyF = {}

        def prepF(grp, ic):
            bi = ic // 8
            ii = ic % 8
            par = bi % 2
            ldtag = ldF[(grp, bi)]
            dq = dqd[ic % 2]
            done = {}
            for fsub in range(2):
                eng = "v" if (ic + fsub) % 3 else "p"
                zc = szdb[grp % 2][:, fsub * 172 + 86 + ic:
                                   fsub * 172 + 86 + ic + 1]
                sc = szdb[grp % 2][:, fsub * 172 + ic: fsub * 172 + ic + 1]
                qsl = qdb[fsub][par][:, ii * 128: ii * 128 + 128]
                dsl = dq[:, fsub * 128:fsub * 128 + 128]
                if eng == "v":
                    for tg in ldtag:
                        dwt(v, tg)
                    wt(v, "v", negzF[grp])
                    wt(v, "t", dqF_war[ic % 2])
                    ins = v.tensor_scalar(dsl, qsl, zc, sc, OP.add, OP.mult)
                    done["v"] = inc("v", ins)
                else:
                    for tg in ldtag:
                        dwt(g, tg)
                    wt(g, "v", negzF[grp])
                    wt(g, "t", dqF_war[ic % 2])
                    ins = g.tensor_scalar(dsl, qsl, zc, sc, OP.add, OP.mult)
                    done["p"] = inc("p", ins)
            b_ = 4 + (ic % 2)
            if tpd_war.get(b_) is not None:
                wt(t, *tpd_war[b_])
            for s_, cnt in done.items():
                wt(t, s_, cnt)
            for fsub in range(2):
                ins = t.matmul(bk(b_, fsub * 128, 128),
                               dq[:, fsub * 128:fsub * 128 + 128],
                               ident[:], start=True, stop=True)
            dqF_war[ic % 2] = inc("t", ins)
            eng2, s2 = (v, "v") if ic % 2 == 0 else (a, "a")
            wt(eng2, "t")
            wd_ = wdt[ic % 2]
            if eng2 is v:
                ins = v.tensor_copy(wd_[:], bk(b_, 0, 256))
            else:
                ins = a.activation(wd_[:], bk(b_, 0, 256), AF.Copy)
            tpd_war[b_] = (s2, inc(s2, ins))
            wreadyF[ic] = tpd_war[b_]

        def mmF(grp, ic):
            bi = ic // 8
            ii = ic % 8
            par = bi % 2
            wd_ = wdt[ic % 2]
            wt(t, *wreadyF.pop(ic))
            for fsub in range(2):
                for th in range(2):
                    b2 = fsub * 2 + th
                    if ic == 0 and dn_bank_war.get(b2) is not None:
                        wt(t, *dn_bank_war[b2])
                    rhs = h_sb[:, ic * 512:(ic + 1) * 512] if th == 0 \
                        else hbb[par][:, ii * 512: ii * 512 + 512]
                    ins = t.matmul(bk(b2),
                                   wd_[:, fsub * 128:fsub * 128 + 128],
                                   rhs, start=(ic == 0), stop=(ic == IC - 1))
            inc("t", ins)
            for fsub in range(2):
                qF_war[(fsub, par)] = N["t"]
            hF_war[par] = N["t"]

        for grp in range(16):
            if grp + 1 < 16:
                loadF_sz(grp + 1)
            dwt(v, ldF[("sz", grp)])
            inc("v", v.tensor_scalar(szdb[grp % 2][:, 86:172],
                                     szdb[grp % 2][:, 86:172], -1.0, None,
                                     OP.mult, OP.bypass))
            ins = v.tensor_scalar(szdb[grp % 2][:, 258:344],
                                  szdb[grp % 2][:, 258:344], -1.0, None,
                                  OP.mult, OP.bypass)
            negzF[grp] = inc("v", ins)
            prepF(grp, 0)
            for ic in range(IC):
                bi = ic // 8
                if ic % 8 == 0:
                    if bi + 1 < nblk:
                        loadF(grp, bi + 1)
                    elif grp + 1 < 16:
                        pass
                if ic + 1 < IC:
                    prepF(grp, ic + 1)
                mmF(grp, ic)
                if ic == IC - 1 and grp + 1 < 16:
                    loadF(grp + 1, 0)
            szF_war[grp % 2] = N["t"]
            for fsub in range(2):
                for th in range(2):
                    b2 = fsub * 2 + th
                    fck = grp * 2 + fsub
                    wt(a, "t")
                    dst = ffn_sb[:, fck * TOK + th * 512: fck * TOK + th * 512 + 512]
                    ins = a.activation(dst, bk(b2), AF.Copy)
                    dn_bank_war[b2] = ("a", inc("a", ins))
        sfree("wdt1", "wdt0", "dqd1", "dqd0", "szdb1", "szdb0",
              "hbb1", "hbb0", "qdb11", "qdb10", "qdb01", "qdb00", "h_sb")
        fence()

        fT0 = salloc("fT0", [128, D], BF16)
        x1t = [salloc("x1t0", [128, D], F32), salloc("x1t1", [128, D], F32)]
        ldG = {}
        x1t_war = [0, 0]
        stG = {}

        def loadG(tc):
            wt(sy, "v", x1t_war[tc % 2])
            dwt(sy, stG.get(tc % 2))
            dwt(sy, x1st_at[tc])
            ldG[tc] = ddma(f"xg{tc % 2}", sy, x1t[tc % 2][:],
                           x1d[tc * 128:tc * 128 + 128, :])

        loadG(0)
        loadG(1)
        ffn3 = ffn_sb.rearrange("p (f w) -> p f w", f=32)
        fdst = fT0.rearrange("p (f w) -> p f w", f=32)
        fT_war = 0
        for tc in range(8):
            for grp in range(8):
                b_ = 4 + grp % 2
                if tp_war.get(b_) is not None:
                    wt(t, *tp_war[b_])
                wt(t, "v", fT_war)
                for k in range(4):
                    f = grp * 4 + k
                    ins = t.matmul(bk(b_, 128 * k, 128),
                                   ffn3[:, f, tc * 128:tc * 128 + 128],
                                   ident[:], start=True, stop=True)
                inc("t", ins)
                eng, s_ = (v, "v") if grp % 2 == 0 else (a, "a")
                wt(eng, "t")
                pv = bk(b_).rearrange("p (k m) -> p k m", k=4)
                if eng is v:
                    ins = v.tensor_copy(fdst[:, grp * 4:grp * 4 + 4, :], pv)
                else:
                    ins = a.activation(fdst[:, grp * 4:grp * 4 + 4, :], pv, AF.Copy)
                tp_war[b_] = (s_, inc(s_, ins))
            wt(v, "a")
            wt(v, "v")
            ln_prep(fT0)
            dwt(v, ldG[tc])
            x_ = x1t[tc % 2]
            inc("v", v.tensor_scalar(x_[:], x_[:], nmr1[:], None, OP.add, OP.bypass))
            ins = v.scalar_tensor_tensor(x_[:], fT0[:], r1[:], x_[:],
                                         OP.mult, OP.add)
            fT_war = inc("v", ins)
            x1t_war[tc % 2] = N["v"]
            wt(g, "v")
            stG[tc % 2] = ddma(f"os{tc % 2}", g,
                               outd[tc * 128:tc * 128 + 128, :], x_[:])
            if tc + 2 < 8:
                loadG(tc + 2)
        sfree("x1t1", "x1t0", "fT0")
        for nm in sorted(dslots):
            sy.wait_ge(dslots[nm][0], dslots[nm][1])

    @blk.sync
    def _(_sync):
        emit()

    st.close()
    return nc


def _prep_inputs(inp):
    f32 = np.float32
    x = np.asarray(inp["L0_attn_x"], f32)
    wq = np.asarray(inp["L0_attn_wq"], f32)
    wk = np.asarray(inp["L0_attn_wk"], f32)
    wv = np.asarray(inp["L0_attn_wv"], f32)
    wo = np.asarray(inp["L0_attn_wo"], f32)
    qg = np.asarray(inp["L0_ffn_qw_gate"], np.int32)
    qu = np.asarray(inp["L0_ffn_qw_up"], np.int32)
    qd = np.asarray(inp["L0_ffn_qw_down"], np.int32)
    sg = np.asarray(inp["L0_ffn_s_gate"], f32)
    zg = np.asarray(inp["L0_ffn_z_gate"], f32)
    su = np.asarray(inp["L0_ffn_s_up"], f32)
    zu = np.asarray(inp["L0_ffn_z_up"], f32)
    sd = np.asarray(inp["L0_ffn_s_down"], f32)
    zd = np.asarray(inp["L0_ffn_z_down"], f32)

    def tile4(wT, nf):
        return np.ascontiguousarray(
            wT.reshape(HC, 128, nf, 128).transpose(2, 1, 0, 3))

    wq4 = tile4(wq.T, NQ)
    wk4 = tile4(wk.T, NKV)
    wv4 = tile4(wv.T, NKV)
    wo4 = np.ascontiguousarray(wo.T.reshape(32, 128, 8, 512).transpose(2, 0, 1, 3))
    qg5 = np.ascontiguousarray(qg.reshape(IC, 128, D))
    qu5 = np.ascontiguousarray(qu.reshape(IC, 128, D))
    qd5 = np.ascontiguousarray(qd.reshape(32, 128, II))
    szgu = np.ascontiguousarray(
        np.stack([sg.reshape(IC, 128, 32), zg.reshape(IC, 128, 32),
                  su.reshape(IC, 128, 32), zu.reshape(IC, 128, 32)], axis=1))
    sd5 = sd.reshape(16, 2, 128, IC)
    zd5 = zd.reshape(16, 2, 128, IC)
    szd_ = np.ascontiguousarray(np.stack([sd5, zd5], axis=2))

    ident = np.eye(128, dtype=ml_dtypes.bfloat16)
    jj = np.arange(640)[None, :]
    pp = np.arange(128)[:, None]
    base = (jj > pp) & (jj <= pp + 512)
    m_even = np.stack([(base & (jj >= 512 - vv * 128)) if vv < 4 else base
                       for vv in range(5)]).astype(ml_dtypes.bfloat16)
    m_odd = np.stack([base] * 5).astype(ml_dtypes.bfloat16)

    in_maps = []
    for core in range(8):
        b, half = divmod(core, 2)
        q0 = half * TOK
        xmain = np.ascontiguousarray(x[b, q0:q0 + TOK])
        xhalo = (np.ascontiguousarray(x[b, q0 - HALO:q0]) if half == 1
                 else np.zeros((HALO, D), f32))
        in_maps.append({
            "xm": xmain, "xh": xhalo,
            "wq4": wq4, "wk4": wk4, "wv4": wv4, "wo4": wo4,
            "qg5": qg5, "qu5": qu5, "qd5": qd5,
            "szgu": szgu, "szd": szd_,
            "maskd": m_odd if half == 1 else m_even,
            "identd": ident,
        })
    return in_maps


def kernel(**inputs):
    in_maps = _prep_inputs(inputs)
    nc = build_core_program()
    res = run_bass_kernel_spmd(nc, in_maps, core_ids=list(range(8)))
    out = np.empty((4, 2048, D), np.float32)
    for core in range(8):
        b, half = divmod(core, 2)
        out[b, half * TOK:(half + 1) * TOK] = res.results[core]["outd"]
    return out
